# version 03
# baseline (speedup 1.0000x reference)
import numpy as np
import ml_dtypes
from contextlib import ExitStack
import concourse.bass as bass
import concourse.mybir as mybir
from concourse.alu_op_type import AluOpType as OP
from concourse.bass_utils import run_bass_kernel_spmd

F32 = mybir.dt.float32
BF16 = mybir.dt.bfloat16
I32 = mybir.dt.int32
AF = mybir.ActivationFunctionType
AX = mybir.AxisListType
BF = ml_dtypes.bfloat16

D = 1024
T = 2048
NCORE = 8
PT = D // 128
TC = 512
NTC = T // TC

ENGS = ("sync", "vector", "scalar", "tensor", "gpsimd")


class Tile:

    __slots__ = ("ap", "root", "writers", "readers")

    def __init__(self, ap, root=None):
        self.ap = ap
        self.root = root if root is not None else self
        if root is None:
            self.writers = {}
            self.readers = []

    def __getitem__(self, idx):
        return Tile(self.ap[idx], self.root)

    def rearrange(self, *a, **k):
        return Tile(self.ap.rearrange(*a, **k), self.root)


class _Op:
    __slots__ = ("eng", "fn", "deps", "idx", "dma", "need", "val", "waits")

    def __init__(self, eng, fn, deps, idx, dma):
        self.eng, self.fn, self.deps, self.idx, self.dma = eng, fn, deps, idx, dma
        self.need = False
        self.val = None
        self.waits = []


class Em:

    def __init__(self):
        self.ops = []
        self.per_eng = {e: [] for e in ENGS}

    def op(self, eng, fn, reads=(), writes=(), dma=False):
        deps = set()
        for t in reads:
            deps.update(t.root.writers.values())
        for t in writes:
            r = t.root
            deps.update(r.writers.values())
            deps.update(r.readers)
        o = _Op(eng, fn, deps, len(self.ops), dma)
        self.ops.append(o)
        self.per_eng[eng].append(o)
        for t in reads:
            t.root.readers.append(o.idx)
        for t in writes:
            r = t.root
            r.writers[(eng, dma)] = o.idx
            r.readers = []
        return o.idx

    def barrier_token(self):
        return len(self.ops)

    def wait_all(self, eng, token=None):
        tok = len(self.ops) if token is None else token
        deps = set(range(tok))
        o = _Op(eng, None, deps, len(self.ops), False)
        self.ops.append(o)
        self.per_eng[eng].append(o)
        return o.idx

    def finalize(self, serial=False):
        if serial:
            prev = None
            for o in self.ops:
                if o.fn is None:
                    continue
                if prev is not None:
                    o.deps = set([prev])
                prev = o.idx
        for o in self.ops:
            for d in o.deps:
                p = self.ops[d]
                if p.fn is None:
                    continue
                if p.eng == o.eng == "tensor":
                    continue
                p.need = True
        val = {e: 0 for e in ENGS}
        dval = {e: 0 for e in ENGS}
        for o in self.ops:
            if o.fn is None:
                continue
            if o.dma:
                o.need = True
                dval[o.eng] += 16
                o.val = dval[o.eng]
            elif o.need:
                val[o.eng] += 1
                o.val = val[o.eng]
        self.totals = {}
        self.dtotals = {}
        for e in ENGS:
            last = None
            for o in reversed(self.per_eng[e]):
                if o.fn is not None and not o.dma:
                    last = o
                    break
            if last is not None and not last.need:
                last.need = True
                val[e] += 1
                last.val = val[e]
            self.totals[e] = val[e]
            self.dtotals[e] = dval[e]
        for e in ENGS:
            seen = {}
            for o in self.per_eng[e]:
                ws = {}
                for d in sorted(o.deps):
                    p = self.ops[d]
                    if p.fn is None or not p.need:
                        continue
                    if p.eng == o.eng == "tensor" and not p.dma and o.fn is not None and not o.dma:
                        continue
                    key = (p.eng, p.dma)
                    if p.val is None:
                        continue
                    ws[key] = max(ws.get(key, 0), p.val)
                for key, v in ws.items():
                    if seen.get(key, 0) >= v:
                        continue
                    seen[key] = v
                    o.waits.append((key, v))

    def emit(self, block, sems, dsems):
        def mk(e):
            def run(eng):
                for o in self.per_eng[e]:
                    for (pe, isdma), v in o.waits:
                        eng.wait_ge(dsems[pe] if isdma else sems[pe], v)
                    if o.fn is None:
                        continue
                    ins = o.fn(eng)
                    if o.need:
                        ins.then_inc(dsems[e] if o.dma else sems[e], 16 if o.dma else 1)
            return run

        for e in ENGS:
            if self.per_eng[e]:
                getattr(block, e)(mk(e))


class Ctx:

    def __init__(self, nc, st):
        self.nc = nc
        self.st = st
        self.em = Em()
        self._uid = 0
        self._scrf = []
        self._scrb = []
        self._scri = []

    def uid(self, p):
        self._uid += 1
        return f"{p}{self._uid}"

    def scr_f(self, i):
        while len(self._scrf) <= i:
            self._scrf.append(self.sb(128, TC, F32))
        return self._scrf[i]

    def scr_b(self, i):
        while len(self._scrb) <= i:
            self._scrb.append(self.sb(128, TC, BF16))
        return self._scrb[i]

    def barrier(self):
        tok = self.em.barrier_token()
        for e in ENGS:
            self.em.wait_all(e, tok)

    def scr_i(self, i):
        while len(self._scri) <= i:
            self._scri.append(self.sb(128, TC, I32))
        return self._scri[i]

    def sb(self, p, f, dt=F32, name=None):
        h = self.st.enter_context(self.nc.sbuf_tensor(name or self.uid("sb"), [p, f], dt))
        return Tile(h.ap())

    def ps(self, p=128, f=512, dt=F32):
        h = self.st.enter_context(self.nc.psum_tensor(self.uid("ps"), [p, f], dt))
        return Tile(h.ap())

    def dram(self, shape, dt=F32, name=None):
        return Tile(self.nc.dram_tensor(name or self.uid("dr"), list(shape), dt).ap())

    def dma(self, out, in_, eng="gpsimd"):
        o, i = (out.ap if isinstance(out, Tile) else out), (in_.ap if isinstance(in_, Tile) else in_)
        r = [in_] if isinstance(in_, Tile) else []
        w = [out] if isinstance(out, Tile) else []
        return self.em.op(eng, lambda E: E.dma_start(out=o, in_=i), reads=r, writes=w, dma=True)

    def mm(self, out, lhsT, rhs, start=True, stop=True):
        return self.em.op("tensor",
                          lambda E: E.matmul(out.ap, lhsT.ap, rhs.ap, start=start, stop=stop),
                          reads=[lhsT, rhs], writes=[out])

    def act(self, out, in_, func, scale=None, bias=None, accum=None, eng="scalar"):
        kw = {}
        if scale is not None:
            kw["scale"] = scale.ap if isinstance(scale, Tile) else np.float32(scale)
        if bias is not None:
            if isinstance(bias, Tile):
                kw["bias"] = bias.ap
            elif func in (AF.Copy, AF.Identity):
                kw["bias"] = float(bias)
            else:
                kw["bias"] = np.float32(bias)
        if accum is not None:
            kw["accum_out"] = accum.ap
        reads = [in_] + [t for t in (scale, bias) if isinstance(t, Tile)]
        writes = [out] + ([accum] if accum is not None else [])
        return self.em.op(eng, lambda E: E.activation(out.ap, in_.ap, func, **kw),
                          reads=reads, writes=writes)

    def tt(self, out, a, b, op, eng="vector"):
        return self.em.op(eng, lambda E: E.tensor_tensor(out=out.ap, in0=a.ap, in1=b.ap, op=op),
                          reads=[a, b], writes=[out])

    def ts(self, out, a, s1, op0, s2=None, op1=None, eng="vector"):
        def f(E):
            kw = {}
            if s2 is not None:
                return E.tensor_scalar(out=out.ap, in0=a.ap,
                                       scalar1=(s1.ap if isinstance(s1, Tile) else s1),
                                       scalar2=(s2.ap if isinstance(s2, Tile) else s2),
                                       op0=op0, op1=op1)
            return E.tensor_scalar(out=out.ap, in0=a.ap,
                                   scalar1=(s1.ap if isinstance(s1, Tile) else s1),
                                   scalar2=None, op0=op0)
        reads = [a] + [t for t in (s1, s2) if isinstance(t, Tile)]
        return self.em.op(eng, f, reads=reads, writes=[out])

    def stt(self, out, in0, scalar, in1, op0=OP.mult, op1=OP.add, eng="vector"):
        s = scalar.ap if isinstance(scalar, Tile) else scalar
        reads = [in0, in1] + ([scalar] if isinstance(scalar, Tile) else [])
        return self.em.op(eng,
                          lambda E: E.scalar_tensor_tensor(out=out.ap, in0=in0.ap, scalar=s,
                                                           in1=in1.ap, op0=op0, op1=op1),
                          reads=reads, writes=[out])

    def scan(self, out, a, b, initial, op0=OP.mult, op1=OP.add):
        init = initial.ap if isinstance(initial, Tile) else initial
        reads = [a, b] + ([initial] if isinstance(initial, Tile) else [])
        return self.em.op("vector",
                          lambda E: E.tensor_tensor_scan(out=out.ap, data0=a.ap, data1=b.ap,
                                                         initial=init, op0=op0, op1=op1),
                          reads=reads, writes=[out])

    def red(self, out, in_, op, absval=False, negate=False, eng="vector"):
        return self.em.op(eng,
                          lambda E: E.tensor_reduce(out=out.ap, in_=in_.ap, axis=AX.X, op=op,
                                                    apply_absolute_value=absval, negate=negate),
                          reads=[in_], writes=[out])

    def copy(self, out, in_, eng="vector"):
        if eng == "scalar":
            return self.em.op(eng, lambda E: E.copy(out=out.ap, in_=in_.ap),
                              reads=[in_], writes=[out])
        return self.em.op(eng, lambda E: E.tensor_copy(out=out.ap, in_=in_.ap),
                          reads=[in_], writes=[out])

    def cast(self, out, in_):
        return self.em.op("scalar", lambda E: E.copy(out=out.ap, in_=in_.ap),
                          reads=[in_], writes=[out])

    def memset(self, out, v, eng="vector"):
        return self.em.op(eng, lambda E: E.memset(out.ap, v), writes=[out])

    def recip(self, out, in_):
        return self.em.op("vector", lambda E: E.reciprocal(out=out.ap, in_=in_.ap),
                          reads=[in_], writes=[out])

    def transpose(self, out_ps, in_, ident):
        return self.em.op("tensor", lambda E: E.transpose(out_ps.ap, in_.ap, ident.ap),
                          reads=[in_, ident], writes=[out_ps])


SERIAL_MODE = True


def build_program(body_fn, in_specs, out_specs, n_banks=8):
    nc = bass.Bass("TRN2", target_bir_lowering=False, debug=False)
    ins = {k: Tile(nc.dram_tensor(k, list(sh), dt, kind="ExternalInput").ap())
           for k, (sh, dt) in in_specs.items()}
    outs = {k: Tile(nc.dram_tensor(k, list(sh), dt, kind="ExternalOutput").ap())
            for k, (sh, dt) in out_specs.items()}
    with ExitStack() as st:
        ctx = Ctx(nc, st)
        ctx.pbanks = [ctx.ps() for _ in range(n_banks)]
        sems = {e: st.enter_context(nc.semaphore(f"s_{e}")) for e in ENGS}
        dsems = {e: st.enter_context(nc.semaphore(f"d_{e}")) for e in ENGS}
        warm = ctx.sb(1, 8, F32)
        ctx.memset(warm, 1.0)
        for f in (AF.Exp, AF.Ln, AF.Silu, AF.Sigmoid, AF.Square, AF.Sqrt, AF.Relu):
            ctx.act(warm, warm, f)
        wdr = ctx.dram((1, 8), F32)
        for _ in range(4):
            ctx.dma(wdr, warm)
            ctx.dma(warm, wdr)
        ctx.barrier()
        body_fn(ctx, ins, outs)
        ctx.em.wait_all("sync")
        ctx.em.finalize(serial=SERIAL_MODE)
        with nc.Block() as block:
            ctx.em.emit(block, sems, dsems)
    return nc



def load_cast_bf16(ctx, dram_t, rows, cols, out_tiles=None, scratch=None):
    nt = rows // 128
    tiles = out_tiles if out_tiles is not None else [ctx.sb(128, cols, BF16) for _ in range(nt)]
    for i in range(nt):
        ctx.dma(tiles[i], dram_t[i * 128:(i + 1) * 128, :], eng="gpsimd")
    return tiles


def layernorm(ctx, x_tiles, ones_col, eps=1e-5, out_dt=BF16, out_tiles=None):
    ps_s, ps_q, ps_b, ps_r = ctx.pbanks[0], ctx.pbanks[1], ctx.pbanks[2], ctx.pbanks[3]
    sq = [ctx.scr_f(0), ctx.scr_f(1)]
    if not hasattr(ctx, "_ln_rows"):
        ctx._ln_rows = [ctx.sb(1, T, F32), ctx.sb(1, T, F32),
                        ctx.sb(1, TC, F32), ctx.sb(1, TC, F32)]
    m_row, rstd_row, v_row, tmp_row = ctx._ln_rows
    t1 = ctx.scr_f(2)
    xn = out_tiles if out_tiles is not None else [ctx.sb(128, T, out_dt) for _ in range(PT)]
    for c in range(NTC):
        sl = slice(c * TC, (c + 1) * TC)
        for p in range(PT):
            ctx.mm(ps_s[0:1, :], ones_col, x_tiles[p][:, sl], start=(p == 0), stop=(p == PT - 1))
        for p in range(PT):
            ctx.act(sq[p % 2], x_tiles[p][:, sl], AF.Square)
            ctx.mm(ps_q[0:1, :], ones_col, sq[p % 2], start=(p == 0), stop=(p == PT - 1))
        ctx.copy(m_row[:, sl], ps_s[0:1, :])
        ctx.act(tmp_row, m_row[:, sl], AF.Square)
        ctx.tt(v_row, ps_q[0:1, :], tmp_row, OP.subtract)
        ctx.ts(v_row, v_row, 0.0, OP.max)
        ctx.act(tmp_row, v_row, AF.Sqrt, bias=eps)
        ctx.recip(rstd_row[:, sl], tmp_row)
        ctx.mm(ps_b, ctx.ones_row_f32, m_row[:, sl])
        ctx.mm(ps_r, ctx.ones_row_f32, rstd_row[:, sl])
        for p in range(PT):
            ctx.tt(t1, x_tiles[p][:, sl], ps_b[:, :], OP.subtract)
            ctx.tt(xn[p][:, sl], t1, ps_r[:, :], OP.mult)
    return xn, m_row, rstd_row


def setup_consts(ctx, ins):
    ctx.ones_row_f32 = ctx.sb(1, 128, F32)
    ones_col = ctx.sb(128, 1, F32)
    ctx.dma(ctx.ones_row_f32, ins["c_ones128"])
    ctx.dma(ones_col, ins["c_onesD"])
    return ones_col


CONST_INPUTS = {
    "c_ones128": np.ones((1, 128), np.float32),
    "c_onesD": np.full((128, 1), 1.0 / D, np.float32),
}


def project(ctx, wtiles, rhs_fn, out_fn, nk=PT, no=PT, ntc=NTC):
    for c in range(ntc):
        for k in range(nk):
            rhs = rhs_fn(k, c)
            for o in range(no):
                ctx.mm(ctx.pbanks[o], wtiles[k][:, o * 128:(o + 1) * 128], rhs,
                       start=(k == 0), stop=(k == nk - 1))
        for o in range(no):
            out_fn(o, c, ctx.pbanks[o])



def body_l1ffn(ctx, ins, outs):
    ones_col = setup_consts(ctx, ins)
    x = [ctx.sb(128, T, F32) for _ in range(PT)]
    for p in range(PT):
        ctx.dma(x[p], ins["x"][p * 128:(p + 1) * 128, :])

    mu_r = ctx.sb(128, PT, F32)
    mu_k = ctx.sb(128, PT, F32)
    ctx.dma(mu_r, ins["mu_r"])
    ctx.dma(mu_k, ins["mu_k"])

    xn_pad = [ctx.sb(128, T + 1, BF16) for _ in range(PT)]
    for p in range(PT):
        ctx.memset(xn_pad[p][:, 0:1], 0.0)
    xn = layernorm(ctx, x, ones_col, out_tiles=[t[:, 1:T + 1] for t in xn_pad])[0]
    l1ffn_core(ctx, x, xn_pad, ins["W_rT"], ins["W_kT"], ins["W_vT"], mu_r, mu_k, scale=0.5)

    for p in range(PT):
        ctx.dma(outs["y"][p * 128:(p + 1) * 128, :], x[p])


def l1ffn_core(ctx, x, xn, WrT_d, WkT_d, WvT_d, mu_r, mu_k, scale=0.5,
               bufs=None):
    if bufs is None:
        bufs = {}
    kact_c = bufs.get("kact_c") or [ctx.sb(128, TC, BF16) for _ in range(PT)]
    r_c = bufs.get("r_c") or [ctx.sb(128, TC, BF16) for _ in range(PT)]
    wk = bufs.get("wk") or [ctx.sb(128, D, BF16) for _ in range(PT)]
    wr = bufs.get("wr") or [ctx.sb(128, D, BF16) for _ in range(PT)]
    wv = bufs.get("wv") or [ctx.sb(128, D, BF16) for _ in range(PT)]
    load_cast_bf16(ctx, WkT_d, D, D, out_tiles=wk)
    load_cast_bf16(ctx, WrT_d, D, D, out_tiles=wr)
    load_cast_bf16(ctx, WvT_d, D, D, out_tiles=wv)

    def blended(mu):
        def f(k, c):
            cur = xn[k][:, 1 + c * TC: 1 + (c + 1) * TC]
            prev = xn[k][:, c * TC: (c + 1) * TC]
            d = ctx.scr_b(k % 2)
            rhs = ctx.scr_b(2 + k % 2)
            ctx.tt(d, cur, prev, OP.subtract)
            ctx.stt(rhs, d, mu[:, k:k + 1], prev, OP.mult, OP.add)
            return rhs
        return f

    bk, br = blended(mu_k), blended(mu_r)
    for c in range(NTC):
        sl = slice(c * TC, (c + 1) * TC)
        for k in range(PT):
            rhs = bk(k, c)
            for o in range(PT):
                ctx.mm(ctx.pbanks[o], wk[k][:, o * 128:(o + 1) * 128], rhs,
                       start=(k == 0), stop=(k == PT - 1))
        for o in range(PT):
            t = ctx.scr_b(4 + o % 2)
            ctx.act(t, ctx.pbanks[o], AF.Relu)
            ctx.act(kact_c[o], t, AF.Square)
        for k in range(PT):
            rhs = br(k, c)
            for o in range(PT):
                ctx.mm(ctx.pbanks[o], wr[k][:, o * 128:(o + 1) * 128], rhs,
                       start=(k == 0), stop=(k == PT - 1))
        for o in range(PT):
            ctx.act(r_c[o], ctx.pbanks[o], AF.Sigmoid)
        for k in range(PT):
            for o in range(PT):
                ctx.mm(ctx.pbanks[o], wv[k][:, o * 128:(o + 1) * 128], kact_c[k],
                       start=(k == 0), stop=(k == PT - 1))
        for o in range(PT):
            vt = ctx.scr_b(4 + o % 2)
            ctx.tt(vt, r_c[o], ctx.pbanks[o], OP.mult)
            ctx.stt(x[o][:, sl], vt, scale, x[o][:, sl], OP.mult, OP.add)



I_FFN = 2816
IT = I_FFN // 128


def absmax_scalar(ctx, tiles, ident_f, ps_bank, out_row):
    cols = ctx.sb(128, len(tiles), F32)
    col = ctx.sb(128, 1, F32)
    for i, t in enumerate(tiles):
        ctx.red(cols[:, i:i + 1], t, OP.max, absval=True)
    ctx.red(col, cols, OP.max)
    ctx.transpose(ps_bank[0:1, 0:128], col, ident_f)
    ctx.red(out_row, ps_bank[0:1, 0:128], OP.max)


def quant_chunks(ctx, dst, src, inv_col, ncols, cw=TC):
    c0 = 0
    while c0 < ncols:
        w = min(cw, ncols - c0)
        sl = slice(c0, c0 + w)
        f = ctx.scr_f(0)
        i = ctx.scr_i(0)
        ctx.ts(f[:, 0:w], src[:, sl], inv_col, OP.mult)
        ctx.copy(i[:, 0:w], f[:, 0:w])
        ctx.ts(dst[:, sl], i[:, 0:w], -128.0, OP.max, 127.0, OP.min)
        c0 += w


def scalar_bcast(ctx, ps_bank, row11, out_col):
    ctx.mm(ps_bank[:, 0:1], ctx.ones_row_f32, row11)
    ctx.copy(out_col, ps_bank[:, 0:1])


def qt_weight(ctx, w_d, wqT_d, sc_d, R, C, ident_b):
    RT, CT = R // 128, C // 128
    wscr = [ctx.sb(128, C, F32) for _ in range(3)]
    wq = [ctx.sb(128, C, BF16) for _ in range(RT)]
    scol = ctx.sb(128, RT, F32)
    mx = ctx.sb(128, 1, F32)
    inv = ctx.sb(128, 1, F32)
    out_sb = ctx.sb(128, R, BF16)
    psT = ctx.ps(128, 128, BF16)
    for rt in range(RT):
        s = wscr[rt % 3]
        ctx.dma(s, w_d[rt * 128:(rt + 1) * 128, :])
        ctx.red(mx, s, OP.max, absval=True)
        ctx.ts(scol[:, rt:rt + 1], mx, 1.0 / 127.0, OP.mult)
        ctx.recip(inv, scol[:, rt:rt + 1])
        quant_chunks(ctx, wq[rt], s, inv, C)
    for j in range(CT):
        for rt in range(RT):
            ctx.transpose(psT, wq[rt][:, j * 128:(j + 1) * 128], ident_b)
            ctx.copy(out_sb[:, rt * 128:(rt + 1) * 128], psT)
        ctx.dma(wqT_d[j * 128:(j + 1) * 128, :], out_sb)
    ctx.dma(sc_d, scol)


def body_ffn0_prep(ctx, ins, outs):
    setup_consts(ctx, ins)
    ident_b = ctx.sb(128, 128, BF16)
    ctx.dma(ident_b, ins["c_eyeb"])
    qt_weight(ctx, ins["w_gate"], outs["wgT"], outs["sg"], I_FFN, D, ident_b)
    qt_weight(ctx, ins["w_up"], outs["wuT"], outs["su"], I_FFN, D, ident_b)
    qt_weight(ctx, ins["w_down"], outs["wdT"], outs["sdn"], D, I_FFN, ident_b)


def body_ffn0_a(ctx, ins, outs):
    ones_col = setup_consts(ctx, ins)
    ident_f = ctx.sb(128, 128, F32)
    ctx.dma(ident_f, ins["c_eyef"])
    x = [ctx.sb(128, T, F32) for _ in range(PT)]
    for p in range(PT):
        ctx.dma(x[p], ins["x"][p * 128:(p + 1) * 128, :])
    xn = layernorm(ctx, x, ones_col, out_dt=F32, out_tiles=x)[0]
    lmax = ctx.sb(1, 1, F32)
    absmax_scalar(ctx, xn, ident_f, ctx.pbanks[4], lmax)
    for p in range(PT):
        ctx.dma(outs["xn"][p * 128:(p + 1) * 128, :], xn[p])
    ctx.dma(outs["lmax"], lmax)


def body_ffn0_b(ctx, ins, outs):
    setup_consts(ctx, ins)
    ident_f = ctx.sb(128, 128, F32)
    ctx.dma(ident_f, ins["c_eyef"])
    xnst = [ctx.sb(128, T, F32) for _ in range(4)]
    xq = [ctx.sb(128, T, BF16) for _ in range(PT)]
    wg = [ctx.sb(128, I_FFN, BF16) for _ in range(PT)]
    wu = [ctx.sb(128, I_FFN, BF16) for _ in range(PT)]
    for k in range(PT):
        ctx.dma(wg[k], ins["wgT"][k * 128:(k + 1) * 128, :])
        ctx.dma(wu[k], ins["wuT"][k * 128:(k + 1) * 128, :])
    sg = ctx.sb(128, IT, F32)
    su = ctx.sb(128, IT, F32)
    ctx.dma(sg, ins["sg"])
    ctx.dma(su, ins["su"])

    g8 = ctx.sb(1, NCORE, F32)
    clip_sb = ctx.sb(1, 1, F32)
    gmax = ctx.sb(1, 1, F32)
    sx = ctx.sb(1, 1, F32)
    inv_sx = ctx.sb(1, 1, F32)
    ctx.dma(g8, ins["gmax8"])
    ctx.dma(clip_sb, ins["clip_x"])
    ctx.red(gmax, g8, OP.max)
    ctx.act(sx, clip_sb, AF.Sigmoid)
    ctx.tt(sx, sx, gmax, OP.mult)
    ctx.ts(sx, sx, 1.0 / 127.0, OP.mult)
    ctx.recip(inv_sx, sx)
    invb = ctx.sb(128, 1, F32)
    sxb = ctx.sb(128, 1, F32)
    scalar_bcast(ctx, ctx.pbanks[4], inv_sx, invb)
    scalar_bcast(ctx, ctx.pbanks[4], sx, sxb)
    sxg = ctx.sb(128, IT, F32)
    sxu = ctx.sb(128, IT, F32)
    ctx.ts(sxg, sg, sxb, OP.mult)
    ctx.ts(sxu, su, sxb, OP.mult)

    for p in range(PT):
        st = xnst[p % 4]
        ctx.dma(st, ins["xn"][p * 128:(p + 1) * 128, :])
        quant_chunks(ctx, xq[p], st, invb, T)

    hmax_cols = ctx.sb(128, IT * NTC, F32)
    for o in range(IT):
        for c in range(NTC):
            sl = slice(c * TC, (c + 1) * TC)
            psg = ctx.pbanks[2 * (c % 2)]
            psu = ctx.pbanks[2 * (c % 2) + 1]
            e1 = ctx.scr_f(c % 2)
            e2 = ctx.scr_f(2 + c % 2)
            for k in range(4):
                ctx.mm(psg, wg[k][:, o * 128:(o + 1) * 128], xq[k][:, sl],
                       start=(k == 0), stop=(k == 3))
            ctx.copy(e1, psg, eng="scalar")
            for k in range(4, PT):
                ctx.mm(psg, wg[k][:, o * 128:(o + 1) * 128], xq[k][:, sl],
                       start=(k == 4), stop=(k == PT - 1))
            ctx.tt(e1, e1, psg, OP.add)
            for k in range(4):
                ctx.mm(psu, wu[k][:, o * 128:(o + 1) * 128], xq[k][:, sl],
                       start=(k == 0), stop=(k == 3))
            ctx.copy(e2, psu, eng="scalar")
            for k in range(4, PT):
                ctx.mm(psu, wu[k][:, o * 128:(o + 1) * 128], xq[k][:, sl],
                       start=(k == 4), stop=(k == PT - 1))
            ctx.tt(e2, e2, psu, OP.add)
            gs = ctx.scr_f(4 + c % 2)
            ht = ctx.scr_f(6 + (o * NTC + c) % 4)
            ctx.act(gs, e1, AF.Silu, scale=sxg[:, o:o + 1])
            ctx.ts(e2, e2, sxu[:, o:o + 1], OP.mult)
            ctx.tt(ht, gs, e2, OP.mult)
            ctx.red(hmax_cols[:, o * NTC + c:o * NTC + c + 1], ht, OP.max, absval=True)
            ctx.dma(outs["h"][o * 128:(o + 1) * 128, sl], ht)
    col = ctx.sb(128, 1, F32)
    lmax = ctx.sb(1, 1, F32)
    ctx.red(col, hmax_cols, OP.max)
    ctx.transpose(ctx.pbanks[4][0:1, 0:128], col, ident_f)
    ctx.red(lmax, ctx.pbanks[4][0:1, 0:128], OP.max)
    ctx.dma(outs["hmax"], lmax)


def body_ffn0_c(ctx, ins, outs):
    setup_consts(ctx, ins)
    sdn = ctx.sb(128, PT, F32)
    ctx.dma(sdn, ins["sdn"])

    g8 = ctx.sb(1, NCORE, F32)
    clip_sb = ctx.sb(1, 1, F32)
    gmax = ctx.sb(1, 1, F32)
    sh = ctx.sb(1, 1, F32)
    inv_sh = ctx.sb(1, 1, F32)
    ctx.dma(g8, ins["hmax8"])
    ctx.dma(clip_sb, ins["clip_down"])
    ctx.red(gmax, g8, OP.max)
    ctx.act(sh, clip_sb, AF.Sigmoid)
    ctx.tt(sh, sh, gmax, OP.mult)
    ctx.ts(sh, sh, 1.0 / 127.0, OP.mult)
    ctx.recip(inv_sh, sh)
    invb = ctx.sb(128, 1, F32)
    shb = ctx.sb(128, 1, F32)
    scalar_bcast(ctx, ctx.pbanks[0], inv_sh, invb)
    scalar_bcast(ctx, ctx.pbanks[0], sh, shb)
    eff = ctx.sb(128, PT, F32)
    ctx.ts(eff, sdn, shb, OP.mult, 0.5, OP.mult)

    wd = [ctx.sb(128, D, BF16) for _ in range(IT)]
    hq = [ctx.sb(128, T, BF16) for _ in range(IT)]
    hst = [ctx.sb(128, T, F32) for _ in range(3)]
    for k in range(IT):
        ctx.dma(wd[k], ins["wdT"][k * 128:(k + 1) * 128, :])
        ctx.dma(hst[k % 3], ins["h"][k * 128:(k + 1) * 128, :])
        quant_chunks(ctx, hq[k], hst[k % 3], invb, T)

    xst = [ctx.sb(128, TC, F32) for _ in range(4)]
    yst = [ctx.sb(128, TC, F32) for _ in range(4)]
    for c in range(NTC):
        sl = slice(c * TC, (c + 1) * TC)
        for k in range(IT):
            for o in range(PT):
                ctx.mm(ctx.pbanks[o], wd[k][:, o * 128:(o + 1) * 128], hq[k][:, sl],
                       start=(k == 0), stop=(k == IT - 1))
        for o in range(PT):
            t = ctx.scr_b(o % 2)
            xc = xst[o % 4]
            yo = yst[o % 4]
            ctx.act(t, ctx.pbanks[o], AF.Copy, scale=eff[:, o:o + 1], bias=0.0)
            ctx.dma(xc, ins["x"][o * 128:(o + 1) * 128, sl])
            ctx.tt(yo, xc, t, OP.add)
            ctx.dma(outs["y"][o * 128:(o + 1) * 128, sl], yo)



INNER = 2048
CTI = INNER // 128


def exp_silu(ctx, out, in_t, sA, sB):
    ctx.act(sA, in_t, AF.Exp, scale=-1.0)
    ctx.ts(sA, sA, 1.0, OP.add)
    ctx.recip(sB, sA)
    ctx.tt(out, in_t, sB, OP.mult)


def body_mamba(ctx, ins, outs):
    ones_col = setup_consts(ctx, ins)
    xd = ins["x"]

    xbuf = [ctx.sb(128, TC, F32) for _ in range(4)]

    def xprov(p, c, bi):
        b = xbuf[bi % 4]
        ctx.dma(b, xd[p * 128:(p + 1) * 128, c * TC:(c + 1) * TC])
        return b

    xn = layernorm_str(ctx, xprov, ones_col)

    cw = ctx.sb(128, CTI * 4, F32)
    cb = ctx.sb(128, CTI, F32)
    Dp = ctx.sb(128, CTI, F32)
    alog = ctx.sb(128, CTI * 16, F32)
    A_sb = ctx.sb(128, CTI * 16, F32)
    ctx.dma(cw, ins["conv_w"])
    ctx.dma(cb, ins["conv_b"])
    ctx.dma(Dp, ins["D"])
    ctx.dma(alog, ins["A_log"])
    ctx.act(A_sb, alog, AF.Exp)
    ctx.ts(A_sb, A_sb, -1.0, OP.mult)

    wsl = [[ctx.sb(128, 256, BF16) for _ in range(PT)] for _ in range(2)]
    y_full = [ctx.sb(128, T, BF16) for _ in range(CTI)]

    xssm = ctx.sb(128, T + 3, BF16)
    xconv = ctx.sb(128, T, BF16)
    delta = ctx.sb(128, T, BF16)
    Bx = ctx.sb(128, T, BF16)
    a_d = ctx.sb(128, T, BF16)
    h_t = ctx.sb(128, T, BF16)
    ysum = ctx.sb(128, T, BF16)
    res_t = ctx.sb(128, T, BF16)
    sres = ctx.sb(128, T, BF16)
    t_a = ctx.sb(128, T, BF16)
    t_b = ctx.sb(128, T, BF16)
    s_f1 = ctx.sb(128, T, F32)
    s_f2 = ctx.sb(128, T, F32)

    ctx.memset(xssm[:, 0:3], 0.0)
    WinT = ins["W_inT"]

    for i in range(CTI):
        ws = wsl[i % 2]
        for k in range(PT):
            ctx.dma(ws[k][:, 0:128], WinT[k * 128:(k + 1) * 128, i * 128:(i + 1) * 128], eng="gpsimd")
            ctx.dma(ws[k][:, 128:256], WinT[k * 128:(k + 1) * 128, INNER + i * 128:INNER + (i + 1) * 128], eng="gpsimd")
        for c in range(NTC):
            sl = slice(c * TC, (c + 1) * TC)
            psA = ctx.pbanks[2 * (c % 2)]
            psB = ctx.pbanks[2 * (c % 2) + 1]
            for k in range(PT):
                ctx.mm(psA, ws[k][:, 0:128], xn[k][:, sl], start=(k == 0), stop=(k == PT - 1))
            for k in range(PT):
                ctx.mm(psB, ws[k][:, 128:256], xn[k][:, sl], start=(k == 0), stop=(k == PT - 1))
            ctx.cast(xssm[:, 3 + c * TC:3 + (c + 1) * TC], psA)
            ctx.cast(res_t[:, sl], psB)
        ctx.ts(t_a, xssm[:, 0:T], cw[:, 4 * i:4 * i + 1], OP.mult)
        for j in range(1, 4):
            ctx.stt(t_a, xssm[:, j:j + T], cw[:, 4 * i + j:4 * i + j + 1], t_a, OP.mult, OP.add)
        ctx.ts(t_b, t_a, cb[:, i:i + 1], OP.add)
        exp_silu(ctx, xconv, t_b, s_f1, s_f2)
        ctx.act(t_a, xconv, AF.Exp)
        ctx.act(delta, t_a, AF.Ln, bias=1.0)
        ctx.act(Bx, xconv, AF.Square)
        for n in range(16):
            ctx.act(a_d, delta, AF.Exp, scale=A_sb[:, i * 16 + n:i * 16 + n + 1])
            ctx.scan(h_t, a_d, Bx, 0.0)
            if n == 0:
                ctx.copy(ysum, h_t)
            else:
                ctx.tt(ysum, ysum, h_t, OP.add)
        exp_silu(ctx, sres, res_t, s_f1, s_f2)
        ctx.tt(h_t, xconv, ysum, OP.mult)
        ctx.tt(h_t, h_t, sres, OP.mult)
        ctx.stt(y_full[i], xconv, Dp[:, i:i + 1], h_t, OP.mult, OP.add)

    WoT = ins["W_outT"]
    wo = [ctx.sb(128, D, BF16) for _ in range(4)]
    for c in range(NTC):
        sl = slice(c * TC, (c + 1) * TC)
        for k in range(CTI):
            ctx.dma(wo[k % 4], WoT[k * 128:(k + 1) * 128, :], eng="gpsimd")
            for o in range(PT):
                ctx.mm(ctx.pbanks[o], wo[k % 4][:, o * 128:(o + 1) * 128], y_full[k][:, sl],
                       start=(k == 0), stop=(k == CTI - 1))
        for o in range(PT):
            xc = xbuf[o % 4]
            ctx.dma(xc, xd[o * 128:(o + 1) * 128, sl])
            yo = ctx.scr_f(o % 4)
            ctx.tt(yo, xc, ctx.pbanks[o], OP.add)
            ctx.dma(outs["y"][o * 128:(o + 1) * 128, sl], yo)


def layernorm_str(ctx, xprov, ones_col, eps=1e-5, out_dt=BF16, out_tiles=None):
    ps_s, ps_q, ps_b, ps_r = ctx.pbanks[0], ctx.pbanks[1], ctx.pbanks[2], ctx.pbanks[3]
    sq = ctx.scr_f(0)
    if not hasattr(ctx, "_ln_rows_c"):
        ctx._ln_rows_c = [ctx.sb(1, TC, F32), ctx.sb(1, TC, F32),
                          ctx.sb(1, TC, F32), ctx.sb(1, TC, F32)]
    m_row, rstd_row, v_row, tmp_row = ctx._ln_rows_c
    xn = out_tiles if out_tiles is not None else [ctx.sb(128, T, out_dt) for _ in range(PT)]
    for c in range(NTC):
        sl = slice(c * TC, (c + 1) * TC)
        for p in range(PT):
            xt = xprov(p, c, p)
            ctx.mm(ps_s[0:1, :], ones_col, xt, start=(p == 0), stop=(p == PT - 1))
            ctx.act(sq, xt, AF.Square)
            ctx.mm(ps_q[0:1, :], ones_col, sq, start=(p == 0), stop=(p == PT - 1))
        ctx.copy(m_row, ps_s[0:1, :])
        ctx.act(tmp_row, m_row, AF.Square)
        ctx.tt(v_row, ps_q[0:1, :], tmp_row, OP.subtract)
        ctx.ts(v_row, v_row, 0.0, OP.max)
        ctx.act(tmp_row, v_row, AF.Sqrt, bias=eps)
        ctx.recip(rstd_row, tmp_row)
        ctx.mm(ps_b, ctx.ones_row_f32, m_row)
        ctx.mm(ps_r, ctx.ones_row_f32, rstd_row)
        for p in range(PT):
            xt = xprov(p, c, p)
            t1 = ctx.scr_f(0)
            ctx.tt(t1, xt, ps_b[:, :], OP.subtract)
            ctx.tt(xn[p][:, sl], t1, ps_r[:, :], OP.mult)
    return xn



NH = 16
HD = 64
NTCH = T // 128


def body_l1attn_a(ctx, ins, outs):
    ones_col = setup_consts(ctx, ins)
    xd = ins["x"]
    xbuf = [ctx.sb(128, TC, F32) for _ in range(4)]

    def xprov(p, c, bi):
        b = xbuf[bi % 4]
        ctx.dma(b, xd[p * 128:(p + 1) * 128, c * TC:(c + 1) * TC])
        return b

    xn = layernorm_str(ctx, xprov, ones_col)

    qw = [ctx.sb(128, D, BF16) for _ in range(PT)]
    load_cast_bf16(ctx, ins["qw"], D, D, out_tiles=qw)

    q = [ctx.sb(128, T, BF16) for _ in range(PT)]
    for c in range(NTC):
        sl = slice(c * TC, (c + 1) * TC)
        for j in range(PT):
            ps = ctx.pbanks[4 + j % 2]
            for k in range(PT):
                ctx.mm(ps, qw[k][:, j * 128:(j + 1) * 128], xn[k][:, sl],
                       start=(k == 0), stop=(k == PT - 1))
            ctx.cast(q[j][:, sl], ps)
    for p in range(PT):
        ctx.dma(outs["q"][p * 128:(p + 1) * 128, :], q[p])

    hpat = ctx.sb(128, NH * PT, BF16)
    ctx.dma(hpat, ins["hpat"])
    psK = ctx.pbanks[6]
    sq = [ctx.sb(128, T, BF16) for _ in range(PT)]
    for j in range(PT):
        ctx.act(sq[j], q[j], AF.Square)
    ksq = ctx.sb(128, NTCH * NH, F32)
    for tc in range(NTCH):
        for j in range(PT):
            ctx.mm(psK[:, 0:NH], sq[j][:, tc * 128:(tc + 1) * 128],
                   hpat[:, j * NH:(j + 1) * NH], start=(j == 0), stop=(j == PT - 1))
        ctx.copy(ksq[:, tc * NH:(tc + 1) * NH], psK[:, 0:NH])
    ctx.dma(outs["ksq"], ksq)


def body_l1attn_b(ctx, ins, outs):
    setup_consts(ctx, ins)
    hpat = ctx.sb(128, NH * PT, BF16)
    ctx.dma(hpat, ins["hpat"])
    q_me = [ctx.sb(128, T, BF16) for _ in range(PT)]
    for p in range(PT):
        ctx.dma(q_me[p], ins["q_me"][p * 128:(p + 1) * 128, :])
    ksq_sb = [ctx.sb(128, NTCH * NH, F32) for _ in range(NCORE)]
    for s in range(NCORE):
        ctx.dma(ksq_sb[s], ins["ksq_all"][s * 128:(s + 1) * 128, :])

    qs = [ctx.sb(128, T, BF16) for _ in range(PT)]
    prod = [ctx.sb(128, T, BF16) for _ in range(PT)]
    Racc = [ctx.sb(128, NTCH * NH, F32) for _ in range(NCORE)]
    psR = ctx.pbanks[0]
    t256 = ctx.sb(128, NTCH * NH, F32)
    for s in range(NCORE):
        for j in range(PT):
            ctx.dma(qs[j], ins["q_all"][s * D + j * 128:s * D + (j + 1) * 128, :])
            ctx.tt(prod[j], q_me[j], qs[j], OP.mult)
        for tc in range(NTCH):
            for j in range(PT):
                ctx.mm(psR[:, 0:NH], prod[j][:, tc * 128:(tc + 1) * 128],
                       hpat[:, j * NH:(j + 1) * NH], start=(j == 0), stop=(j == PT - 1))
            cs = slice(tc * NH, (tc + 1) * NH)
            ctx.ts(t256[:, cs], psR[:, 0:NH], 0.25, OP.mult)
            ctx.stt(Racc[s][:, cs], ksq_sb[s][:, cs], -0.125, t256[:, cs], OP.mult, OP.add)

    ctx.barrier()
    mxp = [ctx.sb(128, NTCH * NH, F32) for _ in range(2)]
    smp = [ctx.sb(128, NTCH * NH, F32) for _ in range(2)]
    rn = ctx.sb(128, NTCH * NH, F32)
    E = [ctx.sb(128, NTCH * NH, BF16) for _ in range(NCORE)]
    P = [ctx.sb(128, NTCH * NH, BF16) for _ in range(NCORE)]
    ctx.tt(mxp[0], Racc[0], Racc[1], OP.max)
    for s in range(2, NCORE):
        ctx.tt(mxp[(s - 1) % 2], mxp[s % 2], Racc[s], OP.max)
    mx = mxp[(NCORE - 2) % 2]
    for s in range(NCORE):
        ctx.tt(t256, Racc[s], mx, OP.subtract)
        ctx.act(E[s], t256, AF.Exp)
    ctx.tt(smp[0], E[0], E[1], OP.add)
    for s in range(2, NCORE):
        ctx.tt(smp[(s - 1) % 2], smp[s % 2], E[s], OP.add)
    ctx.recip(rn, smp[(NCORE - 2) % 2])
    for s in range(NCORE):
        ctx.tt(P[s], E[s], rn, OP.mult)

    ctx.barrier()
    qw = [prod[k][:, 0:D] for k in range(PT)]
    vw = [prod[k][:, D:2 * D] for k in range(PT)]
    for k in range(PT):
        ctx.dma(qw[k], ins["qw"][k * 128:(k + 1) * 128, :], eng="gpsimd")
        ctx.dma(vw[k], ins["vw"][k * 128:(k + 1) * 128, :], eng="gpsimd")
    psM = ctx.pbanks[1]
    M_sb = ctx.sb(64, NH * HD, F32)
    for h in range(NH):
        for k in range(PT):
            ctx.mm(psM[0:HD, 0:HD], qw[k][:, h * HD:(h + 1) * HD],
                   vw[k][:, h * HD:(h + 1) * HD], start=(k == 0), stop=(k == PT - 1))
        ctx.ts(M_sb[0:HD, h * HD:(h + 1) * HD], psM[0:HD, 0:HD], 0.125, OP.mult)
    BD = [ctx.sb(128, 128, BF16) for _ in range(PT)]
    for j in range(PT):
        ctx.memset(BD[j], 0.0)
        h0, h1 = 2 * j, 2 * j + 1
        ctx.dma(BD[j][0:HD, 0:HD], M_sb[0:HD, h0 * HD:(h0 + 1) * HD], eng="gpsimd")
        ctx.dma(BD[j][HD:128, HD:128], M_sb[0:HD, h1 * HD:(h1 + 1) * HD], eng="gpsimd")

    ctx.barrier()
    acc = [ctx.sb(128, D, BF16) for _ in range(NTCH)]
    u_sb = ctx.sb(128, D, BF16)
    tmp = ctx.sb(128, D, BF16)
    for s in range(NCORE):
        for j in range(PT):
            ctx.dma(qs[j], ins["q_all"][s * D + j * 128:s * D + (j + 1) * 128, :])
        for tc in range(NTCH):
            for j in range(PT):
                bank = ctx.pbanks[3 + (j // 4) % 2]
                ctx.mm(bank[:, (j % 4) * 128:(j % 4 + 1) * 128],
                       qs[j][:, tc * 128:(tc + 1) * 128], BD[j],
                       start=True, stop=True)
            ctx.cast(u_sb[:, 0:TC], ctx.pbanks[3])
            ctx.cast(u_sb[:, TC:D], ctx.pbanks[4])
            pb = Tile(P[s].ap[:, tc * NH:(tc + 1) * NH]
                      .rearrange("q (h o) -> q h o", o=1).broadcast_to([128, NH, HD]),
                      P[s].root)
            ctx.tt(Tile(tmp.ap.rearrange("q (h o) -> q h o", o=HD), tmp.root),
                   Tile(u_sb.ap.rearrange("q (h o) -> q h o", o=HD), u_sb.root),
                   pb, OP.mult)
            if s == 0:
                ctx.copy(acc[tc], tmp)
            else:
                ctx.tt(acc[tc], acc[tc], tmp, OP.add)

    ctx.barrier()
    ident_b = ctx.sb(128, 128, BF16)
    ctx.dma(ident_b, ins["c_eyeb"])
    ow = [q_me[k][:, 0:D] for k in range(PT)]
    load_cast_bf16(ctx, ins["out_wT"], D, D, out_tiles=ow)
    ob = ctx.sb(128, PT, F32)
    ctx.dma(ob, ins["out_b"])
    pxv = [qs[j][:, 0:TC] for j in range(PT)]
    psT = ctx.ps(128, 128, BF16)
    xcp = [ctx.sb(128, TC, F32) for _ in range(4)]
    yop = [ctx.sb(128, TC, F32) for _ in range(4)]
    for c in range(NTC):
        sl = slice(c * TC, (c + 1) * TC)
        for tci in range(4):
            tc = c * 4 + tci
            for j in range(PT):
                ctx.transpose(psT, acc[tc][:, j * 128:(j + 1) * 128], ident_b)
                ctx.copy(pxv[j][:, tci * 128:(tci + 1) * 128], psT)
        for o in range(PT):
            ps = ctx.pbanks[5 + o % 2]
            for k in range(PT):
                ctx.mm(ps, ow[k][:, o * 128:(o + 1) * 128], pxv[k],
                       start=(k == 0), stop=(k == PT - 1))
            t = ctx.scr_b(o % 2)
            xc = xcp[o % 4]
            yo = yop[o % 4]
            ctx.ts(t, ps, ob[:, o:o + 1], OP.add)
            ctx.dma(xc, ins["x"][o * 128:(o + 1) * 128, sl])
            ctx.tt(yo, xc, t, OP.add)
            ctx.dma(outs["y"][o * 128:(o + 1) * 128, sl], yo)


def run_l1attn(x_fm_cores, q_weight, v_weight, out_w, out_b):
    qw2d = np.ascontiguousarray(q_weight.reshape(D, D))
    vw2d = np.ascontiguousarray(v_weight.reshape(D, D))
    hpat = np.zeros((128, NH * PT), np.float32)
    for j in range(PT):
        hpat[0:HD, j * NH + 2 * j] = 1.0
        hpat[HD:128, j * NH + 2 * j + 1] = 1.0
    hpat = hpat.astype(BF)

    spec_in = {"x": ((D, T), F32), "qw": ((D, D), F32), "hpat": ((128, NH * PT), BF16),
               **_spec_consts()}
    spec_out = {"q": ((D, T), BF16), "ksq": ((128, NTCH * NH), F32)}
    nc = build_program(body_l1attn_a, spec_in, spec_out)
    shared = {"qw": qw2d, "hpat": hpat, **CONST_INPUTS}
    res = run_bass_kernel_spmd(nc, [{"x": xc, **shared} for xc in x_fm_cores],
                               core_ids=list(range(NCORE)))
    q_cores = [r["q"] for r in res.results]
    ksq_all = np.concatenate([r["ksq"] for r in res.results], axis=0)
    q_all = np.concatenate(q_cores, axis=0)

    spec_in = {"x": ((D, T), F32), "q_me": ((D, T), BF16),
               "q_all": ((NCORE * D, T), BF16), "ksq_all": ((NCORE * 128, NTCH * NH), F32),
               "qw": ((D, D), F32), "vw": ((D, D), F32), "out_wT": ((D, D), F32),
               "out_b": ((128, PT), F32), "hpat": ((128, NH * PT), BF16),
               "c_eyeb": ((128, 128), BF16), **_spec_consts()}
    spec_out = {"y": ((D, T), F32)}
    nc = build_program(body_l1attn_b, spec_in, spec_out, n_banks=7)
    shared = {"q_all": q_all, "ksq_all": ksq_all, "qw": qw2d, "vw": vw2d,
              "out_wT": np.ascontiguousarray(out_w.T), "out_b": _ppn(out_b, PT),
              "hpat": hpat, "c_eyeb": CONST_INPUTS_EYE["c_eyeb"], **CONST_INPUTS}
    res = run_bass_kernel_spmd(
        nc, [{"x": xc, "q_me": qc, **shared} for xc, qc in zip(x_fm_cores, q_cores)],
        core_ids=list(range(NCORE)))
    return [r["y"] for r in res.results]



def _fm(a):
    return np.ascontiguousarray(a.T)


def _pp(v):
    return np.ascontiguousarray(v.reshape(PT, 128).T)


def run_l1ffn(x_fm_cores, W_r, W_k, W_v, mu_r, mu_k):
    spec_in = {
        "x": ((D, T), F32), "W_rT": ((D, D), F32), "W_kT": ((D, D), F32),
        "W_vT": ((D, D), F32), "mu_r": ((128, PT), F32), "mu_k": ((128, PT), F32),
        "c_ones128": ((1, 128), F32), "c_onesD": ((128, 1), F32),
    }
    spec_out = {"y": ((D, T), F32)}
    nc = build_program(body_l1ffn, spec_in, spec_out)
    shared = {
        "W_rT": np.ascontiguousarray(W_r.T), "W_kT": np.ascontiguousarray(W_k.T),
        "W_vT": np.ascontiguousarray(W_v.T), "mu_r": _pp(mu_r), "mu_k": _pp(mu_k),
        **CONST_INPUTS,
    }
    in_maps = [{"x": xc, **shared} for xc in x_fm_cores]
    res = run_bass_kernel_spmd(nc, in_maps, core_ids=list(range(NCORE)))
    return [r["y"] for r in res.results]


def run_mamba(x_fm_cores, W_in, conv_w, conv_b, A_log, Dv, W_out):
    spec_in = {
        "x": ((D, T), F32), "W_inT": ((D, 2 * INNER), F32),
        "W_outT": ((INNER, D), F32), "conv_w": ((128, CTI * 4), F32),
        "conv_b": ((128, CTI), F32), "D": ((128, CTI), F32),
        "A_log": ((128, CTI * 16), F32), **_spec_consts(),
    }
    spec_out = {"y": ((D, T), F32)}
    nc = build_program(body_mamba, spec_in, spec_out)
    shared = {
        "W_inT": np.ascontiguousarray(W_in.T),
        "W_outT": np.ascontiguousarray(W_out.T),
        "conv_w": np.ascontiguousarray(conv_w.reshape(CTI, 128, 4).transpose(1, 0, 2).reshape(128, CTI * 4)),
        "conv_b": _ppn(conv_b, CTI), "D": _ppn(Dv, CTI),
        "A_log": np.ascontiguousarray(A_log.reshape(CTI, 128, 16).transpose(1, 0, 2).reshape(128, CTI * 16)),
        **CONST_INPUTS,
    }
    in_maps = [{"x": xc, **shared} for xc in x_fm_cores]
    res = run_bass_kernel_spmd(nc, in_maps, core_ids=list(range(NCORE)))
    return [r["y"] for r in res.results]


def _ppn(v, n):
    return np.ascontiguousarray(v.reshape(n, 128).T)


CONST_INPUTS_EYE = {
    "c_eyef": np.eye(128, dtype=np.float32),
    "c_eyeb": np.eye(128, dtype=BF),
}


def _spec_consts():
    return {"c_ones128": ((1, 128), F32), "c_onesD": ((128, 1), F32)}


def run_ffn0_prep(w_gate, w_up, w_down):
    spec_in = {"w_gate": ((I_FFN, D), F32), "w_up": ((I_FFN, D), F32),
               "w_down": ((D, I_FFN), F32), "c_eyeb": ((128, 128), BF16),
               **_spec_consts()}
    spec_out = {"wgT": ((D, I_FFN), BF16), "wuT": ((D, I_FFN), BF16),
                "wdT": ((I_FFN, D), BF16),
                "sg": ((128, IT), F32), "su": ((128, IT), F32), "sdn": ((128, PT), F32)}
    nc = build_program(body_ffn0_prep, spec_in, spec_out, n_banks=4)
    shared = {"w_gate": w_gate, "w_up": w_up, "w_down": w_down,
              "c_eyeb": CONST_INPUTS_EYE["c_eyeb"], **CONST_INPUTS}
    res = run_bass_kernel_spmd(nc, [dict(shared) for _ in range(NCORE)],
                               core_ids=list(range(NCORE)))
    return res.results[0]


def run_ffn0(x_fm_cores, prep, clip_x, clip_down):
    spec_in = {"x": ((D, T), F32), "c_eyef": ((128, 128), F32), **_spec_consts()}
    spec_out = {"xn": ((D, T), F32), "lmax": ((1, 1), F32)}
    nc = build_program(body_ffn0_a, spec_in, spec_out)
    shared = {"c_eyef": CONST_INPUTS_EYE["c_eyef"], **CONST_INPUTS}
    res = run_bass_kernel_spmd(nc, [{"x": xc, **shared} for xc in x_fm_cores],
                               core_ids=list(range(NCORE)))
    xns = [r["xn"] for r in res.results]
    gmax8 = np.concatenate([r["lmax"] for r in res.results], axis=1)

    spec_in = {"xn": ((D, T), F32), "gmax8": ((1, NCORE), F32),
               "clip_x": ((1, 1), F32), "wgT": ((D, I_FFN), BF16),
               "wuT": ((D, I_FFN), BF16), "sg": ((128, IT), F32),
               "su": ((128, IT), F32), "c_eyef": ((128, 128), F32), **_spec_consts()}
    spec_out = {"h": ((I_FFN, T), F32), "hmax": ((1, 1), F32)}
    nc = build_program(body_ffn0_b, spec_in, spec_out)
    shared = {"gmax8": gmax8, "clip_x": clip_x.reshape(1, 1),
              "wgT": prep["wgT"], "wuT": prep["wuT"], "sg": prep["sg"],
              "su": prep["su"], "c_eyef": CONST_INPUTS_EYE["c_eyef"], **CONST_INPUTS}
    res = run_bass_kernel_spmd(nc, [{"xn": xn, **shared} for xn in xns],
                               core_ids=list(range(NCORE)))
    hs = [r["h"] for r in res.results]
    hmax8 = np.concatenate([r["hmax"] for r in res.results], axis=1)

    spec_in = {"x": ((D, T), F32), "h": ((I_FFN, T), F32),
               "hmax8": ((1, NCORE), F32), "clip_down": ((1, 1), F32),
               "wdT": ((I_FFN, D), BF16), "sdn": ((128, PT), F32), **_spec_consts()}
    spec_out = {"y": ((D, T), F32)}
    nc = build_program(body_ffn0_c, spec_in, spec_out)
    shared = {"hmax8": hmax8, "clip_down": clip_down.reshape(1, 1),
              "wdT": prep["wdT"], "sdn": prep["sdn"], **CONST_INPUTS}
    res = run_bass_kernel_spmd(
        nc, [{"x": xc, "h": h, **shared} for xc, h in zip(x_fm_cores, hs)],
        core_ids=list(range(NCORE)))
    return [r["y"] for r in res.results]


def kernel(**inputs):
    x = np.ascontiguousarray(inputs["L0_attn_x"], dtype=np.float32)
    x_fm = [_fm(x[b]) for b in range(NCORE)]

    prep = run_ffn0_prep(np.ascontiguousarray(inputs["L0_ffn_w_gate"], dtype=np.float32),
                         np.ascontiguousarray(inputs["L0_ffn_w_up"], dtype=np.float32),
                         np.ascontiguousarray(inputs["L0_ffn_w_down"], dtype=np.float32))
    clip_x = np.asarray(inputs["L0_ffn_clip_x"], dtype=np.float32)
    clip_down = np.asarray(inputs["L0_ffn_clip_down"], dtype=np.float32)

    x_fm = run_ffn0(x_fm, prep, clip_x, clip_down)
    x_fm = run_mamba(x_fm,
                     np.ascontiguousarray(inputs["L0_attn_W_in"], dtype=np.float32),
                     np.ascontiguousarray(inputs["L0_attn_conv_w"], dtype=np.float32),
                     np.ascontiguousarray(inputs["L0_attn_conv_b"], dtype=np.float32),
                     np.ascontiguousarray(inputs["L0_attn_A_log"], dtype=np.float32),
                     np.ascontiguousarray(inputs["L0_attn_D"], dtype=np.float32),
                     np.ascontiguousarray(inputs["L0_attn_W_out"], dtype=np.float32))
    x_fm = run_ffn0(x_fm, prep, clip_x, clip_down)

    l1f = (np.ascontiguousarray(inputs["L1_ffn_W_r"], dtype=np.float32),
           np.ascontiguousarray(inputs["L1_ffn_W_k"], dtype=np.float32),
           np.ascontiguousarray(inputs["L1_ffn_W_v"], dtype=np.float32),
           np.ascontiguousarray(inputs["L1_ffn_mu_r"], dtype=np.float32),
           np.ascontiguousarray(inputs["L1_ffn_mu_k"], dtype=np.float32))
    x_fm = run_l1ffn(x_fm, *l1f)
    x_fm = run_l1attn(x_fm,
                      np.ascontiguousarray(inputs["L1_attn_q_weight"], dtype=np.float32),
                      np.ascontiguousarray(inputs["L1_attn_v_weight"], dtype=np.float32),
                      np.ascontiguousarray(inputs["L1_attn_out_w"], dtype=np.float32),
                      np.ascontiguousarray(inputs["L1_attn_out_b"], dtype=np.float32))
    x_fm = run_l1ffn(x_fm, *l1f)

    return np.stack([np.asarray(y, dtype=np.float32).T for y in x_fm], axis=0)


# version 04
# speedup vs baseline: 1.0106x; 1.0106x over previous
import numpy as np
import ml_dtypes
from contextlib import ExitStack
import concourse.bass as bass
import concourse.mybir as mybir
from concourse.alu_op_type import AluOpType as OP
from concourse.bass_utils import run_bass_kernel_spmd

F32 = mybir.dt.float32
BF16 = mybir.dt.bfloat16
I32 = mybir.dt.int32
AF = mybir.ActivationFunctionType
AX = mybir.AxisListType
BF = ml_dtypes.bfloat16

D = 1024
T = 2048
NCORE = 8
PT = D // 128
TC = 512
NTC = T // TC

ENGS = ("sync", "vector", "scalar", "tensor", "gpsimd")


class Tile:

    __slots__ = ("ap", "root", "writers", "readers")

    def __init__(self, ap, root=None):
        self.ap = ap
        self.root = root if root is not None else self
        if root is None:
            self.writers = {}
            self.readers = []

    def __getitem__(self, idx):
        return Tile(self.ap[idx], self.root)

    def rearrange(self, *a, **k):
        return Tile(self.ap.rearrange(*a, **k), self.root)


class _Op:
    __slots__ = ("eng", "fn", "deps", "idx", "dma", "need", "val", "waits")

    def __init__(self, eng, fn, deps, idx, dma):
        self.eng, self.fn, self.deps, self.idx, self.dma = eng, fn, deps, idx, dma
        self.need = False
        self.val = None
        self.waits = []


class Em:

    def __init__(self):
        self.ops = []
        self.per_eng = {e: [] for e in ENGS}

    def op(self, eng, fn, reads=(), writes=(), dma=False):
        deps = set()
        for t in reads:
            deps.update(t.root.writers.values())
        for t in writes:
            r = t.root
            deps.update(r.writers.values())
            deps.update(r.readers)
        o = _Op(eng, fn, deps, len(self.ops), dma)
        self.ops.append(o)
        self.per_eng[eng].append(o)
        for t in reads:
            t.root.readers.append(o.idx)
        for t in writes:
            r = t.root
            r.writers[(eng, dma)] = o.idx
            r.readers = []
        return o.idx

    def barrier_token(self):
        return len(self.ops)

    def wait_all(self, eng, token=None):
        tok = len(self.ops) if token is None else token
        deps = set(range(tok))
        o = _Op(eng, None, deps, len(self.ops), False)
        self.ops.append(o)
        self.per_eng[eng].append(o)
        return o.idx

    def finalize(self, serial=False):
        if serial:
            prev = None
            prev_tensor = False
            for o in self.ops:
                if o.fn is None:
                    continue
                is_t = (o.eng == "tensor" and not o.dma)
                if prev is not None:
                    if is_t and prev_tensor:
                        o.deps = set()
                    else:
                        o.deps = set([prev])
                prev = o.idx
                prev_tensor = is_t
        for o in self.ops:
            for d in o.deps:
                p = self.ops[d]
                if p.fn is None:
                    continue
                if p.eng == o.eng == "tensor":
                    continue
                p.need = True
        val = {e: 0 for e in ENGS}
        dval = {e: 0 for e in ENGS}
        for o in self.ops:
            if o.fn is None:
                continue
            if o.dma:
                o.need = True
                dval[o.eng] += 16
                o.val = dval[o.eng]
            elif o.need:
                val[o.eng] += 1
                o.val = val[o.eng]
        self.totals = {}
        self.dtotals = {}
        for e in ENGS:
            last = None
            for o in reversed(self.per_eng[e]):
                if o.fn is not None and not o.dma:
                    last = o
                    break
            if last is not None and not last.need:
                last.need = True
                val[e] += 1
                last.val = val[e]
            self.totals[e] = val[e]
            self.dtotals[e] = dval[e]
        for e in ENGS:
            seen = {}
            for o in self.per_eng[e]:
                ws = {}
                for d in sorted(o.deps):
                    p = self.ops[d]
                    if p.fn is None or not p.need:
                        continue
                    if p.eng == o.eng == "tensor" and not p.dma and o.fn is not None and not o.dma:
                        continue
                    key = (p.eng, p.dma)
                    if p.val is None:
                        continue
                    ws[key] = max(ws.get(key, 0), p.val)
                for key, v in ws.items():
                    if seen.get(key, 0) >= v:
                        continue
                    seen[key] = v
                    o.waits.append((key, v))

    def emit(self, block, sems, dsems):
        def mk(e):
            def run(eng):
                for o in self.per_eng[e]:
                    for (pe, isdma), v in o.waits:
                        eng.wait_ge(dsems[pe] if isdma else sems[pe], v)
                    if o.fn is None:
                        continue
                    ins = o.fn(eng)
                    if o.need:
                        ins.then_inc(dsems[e] if o.dma else sems[e], 16 if o.dma else 1)
            return run

        for e in ENGS:
            if self.per_eng[e]:
                getattr(block, e)(mk(e))


class Ctx:

    def __init__(self, nc, st):
        self.nc = nc
        self.st = st
        self.em = Em()
        self._uid = 0
        self._scrf = []
        self._scrb = []
        self._scri = []

    def uid(self, p):
        self._uid += 1
        return f"{p}{self._uid}"

    def scr_f(self, i):
        while len(self._scrf) <= i:
            self._scrf.append(self.sb(128, TC, F32))
        return self._scrf[i]

    def scr_b(self, i):
        while len(self._scrb) <= i:
            self._scrb.append(self.sb(128, TC, BF16))
        return self._scrb[i]

    def barrier(self):
        tok = self.em.barrier_token()
        for e in ENGS:
            self.em.wait_all(e, tok)

    def scr_i(self, i):
        while len(self._scri) <= i:
            self._scri.append(self.sb(128, TC, I32))
        return self._scri[i]

    def sb(self, p, f, dt=F32, name=None):
        h = self.st.enter_context(self.nc.sbuf_tensor(name or self.uid("sb"), [p, f], dt))
        return Tile(h.ap())

    def ps(self, p=128, f=512, dt=F32):
        h = self.st.enter_context(self.nc.psum_tensor(self.uid("ps"), [p, f], dt))
        return Tile(h.ap())

    def dram(self, shape, dt=F32, name=None):
        return Tile(self.nc.dram_tensor(name or self.uid("dr"), list(shape), dt).ap())

    def dma(self, out, in_, eng="gpsimd"):
        o, i = (out.ap if isinstance(out, Tile) else out), (in_.ap if isinstance(in_, Tile) else in_)
        r = [in_] if isinstance(in_, Tile) else []
        w = [out] if isinstance(out, Tile) else []
        return self.em.op(eng, lambda E: E.dma_start(out=o, in_=i), reads=r, writes=w, dma=True)

    def mm(self, out, lhsT, rhs, start=True, stop=True):
        return self.em.op("tensor",
                          lambda E: E.matmul(out.ap, lhsT.ap, rhs.ap, start=start, stop=stop),
                          reads=[lhsT, rhs], writes=[out])

    def act(self, out, in_, func, scale=None, bias=None, accum=None, eng="scalar"):
        kw = {}
        if scale is not None:
            kw["scale"] = scale.ap if isinstance(scale, Tile) else np.float32(scale)
        if bias is not None:
            if isinstance(bias, Tile):
                kw["bias"] = bias.ap
            elif func in (AF.Copy, AF.Identity):
                kw["bias"] = float(bias)
            else:
                kw["bias"] = np.float32(bias)
        if accum is not None:
            kw["accum_out"] = accum.ap
        reads = [in_] + [t for t in (scale, bias) if isinstance(t, Tile)]
        writes = [out] + ([accum] if accum is not None else [])
        return self.em.op(eng, lambda E: E.activation(out.ap, in_.ap, func, **kw),
                          reads=reads, writes=writes)

    def tt(self, out, a, b, op, eng="vector"):
        return self.em.op(eng, lambda E: E.tensor_tensor(out=out.ap, in0=a.ap, in1=b.ap, op=op),
                          reads=[a, b], writes=[out])

    def ts(self, out, a, s1, op0, s2=None, op1=None, eng="vector"):
        def f(E):
            kw = {}
            if s2 is not None:
                return E.tensor_scalar(out=out.ap, in0=a.ap,
                                       scalar1=(s1.ap if isinstance(s1, Tile) else s1),
                                       scalar2=(s2.ap if isinstance(s2, Tile) else s2),
                                       op0=op0, op1=op1)
            return E.tensor_scalar(out=out.ap, in0=a.ap,
                                   scalar1=(s1.ap if isinstance(s1, Tile) else s1),
                                   scalar2=None, op0=op0)
        reads = [a] + [t for t in (s1, s2) if isinstance(t, Tile)]
        return self.em.op(eng, f, reads=reads, writes=[out])

    def stt(self, out, in0, scalar, in1, op0=OP.mult, op1=OP.add, eng="vector"):
        s = scalar.ap if isinstance(scalar, Tile) else scalar
        reads = [in0, in1] + ([scalar] if isinstance(scalar, Tile) else [])
        return self.em.op(eng,
                          lambda E: E.scalar_tensor_tensor(out=out.ap, in0=in0.ap, scalar=s,
                                                           in1=in1.ap, op0=op0, op1=op1),
                          reads=reads, writes=[out])

    def scan(self, out, a, b, initial, op0=OP.mult, op1=OP.add):
        init = initial.ap if isinstance(initial, Tile) else initial
        reads = [a, b] + ([initial] if isinstance(initial, Tile) else [])
        return self.em.op("vector",
                          lambda E: E.tensor_tensor_scan(out=out.ap, data0=a.ap, data1=b.ap,
                                                         initial=init, op0=op0, op1=op1),
                          reads=reads, writes=[out])

    def red(self, out, in_, op, absval=False, negate=False, eng="vector"):
        return self.em.op(eng,
                          lambda E: E.tensor_reduce(out=out.ap, in_=in_.ap, axis=AX.X, op=op,
                                                    apply_absolute_value=absval, negate=negate),
                          reads=[in_], writes=[out])

    def copy(self, out, in_, eng="vector"):
        if eng == "scalar":
            return self.em.op(eng, lambda E: E.copy(out=out.ap, in_=in_.ap),
                              reads=[in_], writes=[out])
        return self.em.op(eng, lambda E: E.tensor_copy(out=out.ap, in_=in_.ap),
                          reads=[in_], writes=[out])

    def cast(self, out, in_):
        return self.em.op("scalar", lambda E: E.copy(out=out.ap, in_=in_.ap),
                          reads=[in_], writes=[out])

    def memset(self, out, v, eng="vector"):
        return self.em.op(eng, lambda E: E.memset(out.ap, v), writes=[out])

    def recip(self, out, in_):
        return self.em.op("vector", lambda E: E.reciprocal(out=out.ap, in_=in_.ap),
                          reads=[in_], writes=[out])

    def transpose(self, out_ps, in_, ident):
        return self.em.op("tensor", lambda E: E.transpose(out_ps.ap, in_.ap, ident.ap),
                          reads=[in_, ident], writes=[out_ps])


SERIAL_MODE = True


def build_program(body_fn, in_specs, out_specs, n_banks=8):
    nc = bass.Bass("TRN2", target_bir_lowering=False, debug=False)
    ins = {k: Tile(nc.dram_tensor(k, list(sh), dt, kind="ExternalInput").ap())
           for k, (sh, dt) in in_specs.items()}
    outs = {k: Tile(nc.dram_tensor(k, list(sh), dt, kind="ExternalOutput").ap())
            for k, (sh, dt) in out_specs.items()}
    with ExitStack() as st:
        ctx = Ctx(nc, st)
        ctx.pbanks = [ctx.ps() for _ in range(n_banks)]
        sems = {e: st.enter_context(nc.semaphore(f"s_{e}")) for e in ENGS}
        dsems = {e: st.enter_context(nc.semaphore(f"d_{e}")) for e in ENGS}
        warm = ctx.sb(1, 8, F32)
        ctx.memset(warm, 1.0)
        for f in (AF.Exp, AF.Ln, AF.Silu, AF.Sigmoid, AF.Square, AF.Sqrt, AF.Relu):
            ctx.act(warm, warm, f)
        wdr = ctx.dram((1, 8), F32)
        for _ in range(4):
            ctx.dma(wdr, warm)
            ctx.dma(warm, wdr)
        ctx.barrier()
        body_fn(ctx, ins, outs)
        ctx.em.wait_all("sync")
        ctx.em.finalize(serial=SERIAL_MODE)
        with nc.Block() as block:
            ctx.em.emit(block, sems, dsems)
    return nc



def load_cast_bf16(ctx, dram_t, rows, cols, out_tiles=None, scratch=None):
    nt = rows // 128
    tiles = out_tiles if out_tiles is not None else [ctx.sb(128, cols, BF16) for _ in range(nt)]
    for i in range(nt):
        ctx.dma(tiles[i], dram_t[i * 128:(i + 1) * 128, :], eng="gpsimd")
    return tiles


def layernorm(ctx, x_tiles, ones_col, eps=1e-5, out_dt=BF16, out_tiles=None):
    ps_s, ps_q, ps_b, ps_r = ctx.pbanks[0], ctx.pbanks[1], ctx.pbanks[2], ctx.pbanks[3]
    sq = [ctx.scr_f(0), ctx.scr_f(1)]
    if not hasattr(ctx, "_ln_rows"):
        ctx._ln_rows = [ctx.sb(1, T, F32), ctx.sb(1, T, F32),
                        ctx.sb(1, TC, F32), ctx.sb(1, TC, F32)]
    m_row, rstd_row, v_row, tmp_row = ctx._ln_rows
    t1 = ctx.scr_f(2)
    xn = out_tiles if out_tiles is not None else [ctx.sb(128, T, out_dt) for _ in range(PT)]
    for c in range(NTC):
        sl = slice(c * TC, (c + 1) * TC)
        for p in range(PT):
            ctx.mm(ps_s[0:1, :], ones_col, x_tiles[p][:, sl], start=(p == 0), stop=(p == PT - 1))
        for p in range(PT):
            ctx.act(sq[p % 2], x_tiles[p][:, sl], AF.Square)
            ctx.mm(ps_q[0:1, :], ones_col, sq[p % 2], start=(p == 0), stop=(p == PT - 1))
        ctx.copy(m_row[:, sl], ps_s[0:1, :])
        ctx.act(tmp_row, m_row[:, sl], AF.Square)
        ctx.tt(v_row, ps_q[0:1, :], tmp_row, OP.subtract)
        ctx.ts(v_row, v_row, 0.0, OP.max)
        ctx.act(tmp_row, v_row, AF.Sqrt, bias=eps)
        ctx.recip(rstd_row[:, sl], tmp_row)
        ctx.mm(ps_b, ctx.ones_row_f32, m_row[:, sl])
        ctx.mm(ps_r, ctx.ones_row_f32, rstd_row[:, sl])
        for p in range(PT):
            ctx.tt(t1, x_tiles[p][:, sl], ps_b[:, :], OP.subtract)
            ctx.tt(xn[p][:, sl], t1, ps_r[:, :], OP.mult)
    return xn, m_row, rstd_row


def setup_consts(ctx, ins):
    ctx.ones_row_f32 = ctx.sb(1, 128, F32)
    ones_col = ctx.sb(128, 1, F32)
    ctx.dma(ctx.ones_row_f32, ins["c_ones128"])
    ctx.dma(ones_col, ins["c_onesD"])
    return ones_col


CONST_INPUTS = {
    "c_ones128": np.ones((1, 128), np.float32),
    "c_onesD": np.full((128, 1), 1.0 / D, np.float32),
}


def project(ctx, wtiles, rhs_fn, out_fn, nk=PT, no=PT, ntc=NTC):
    for c in range(ntc):
        for k in range(nk):
            rhs = rhs_fn(k, c)
            for o in range(no):
                ctx.mm(ctx.pbanks[o], wtiles[k][:, o * 128:(o + 1) * 128], rhs,
                       start=(k == 0), stop=(k == nk - 1))
        for o in range(no):
            out_fn(o, c, ctx.pbanks[o])



def body_l1ffn(ctx, ins, outs):
    ones_col = setup_consts(ctx, ins)
    x = [ctx.sb(128, T, F32) for _ in range(PT)]
    for p in range(PT):
        ctx.dma(x[p], ins["x"][p * 128:(p + 1) * 128, :])

    mu_r = ctx.sb(128, PT, F32)
    mu_k = ctx.sb(128, PT, F32)
    ctx.dma(mu_r, ins["mu_r"])
    ctx.dma(mu_k, ins["mu_k"])

    xn_pad = [ctx.sb(128, T + 1, BF16) for _ in range(PT)]
    for p in range(PT):
        ctx.memset(xn_pad[p][:, 0:1], 0.0)
    xn = layernorm(ctx, x, ones_col, out_tiles=[t[:, 1:T + 1] for t in xn_pad])[0]
    l1ffn_core(ctx, x, xn_pad, ins["W_rT"], ins["W_kT"], ins["W_vT"], mu_r, mu_k, scale=0.5)

    for p in range(PT):
        ctx.dma(outs["y"][p * 128:(p + 1) * 128, :], x[p])


def l1ffn_core(ctx, x, xn, WrT_d, WkT_d, WvT_d, mu_r, mu_k, scale=0.5,
               bufs=None):
    if bufs is None:
        bufs = {}
    kact_c = bufs.get("kact_c") or [ctx.sb(128, TC, BF16) for _ in range(PT)]
    r_c = bufs.get("r_c") or [ctx.sb(128, TC, BF16) for _ in range(PT)]
    wk = bufs.get("wk") or [ctx.sb(128, D, BF16) for _ in range(PT)]
    wr = bufs.get("wr") or [ctx.sb(128, D, BF16) for _ in range(PT)]
    wv = bufs.get("wv") or [ctx.sb(128, D, BF16) for _ in range(PT)]
    load_cast_bf16(ctx, WkT_d, D, D, out_tiles=wk)
    load_cast_bf16(ctx, WrT_d, D, D, out_tiles=wr)
    load_cast_bf16(ctx, WvT_d, D, D, out_tiles=wv)

    def blended(mu):
        def f(k, c):
            cur = xn[k][:, 1 + c * TC: 1 + (c + 1) * TC]
            prev = xn[k][:, c * TC: (c + 1) * TC]
            d = ctx.scr_b(k % 2)
            rhs = ctx.scr_b(2 + k % 2)
            ctx.tt(d, cur, prev, OP.subtract)
            ctx.stt(rhs, d, mu[:, k:k + 1], prev, OP.mult, OP.add)
            return rhs
        return f

    bk, br = blended(mu_k), blended(mu_r)
    for c in range(NTC):
        sl = slice(c * TC, (c + 1) * TC)
        for k in range(PT):
            rhs = bk(k, c)
            for o in range(PT):
                ctx.mm(ctx.pbanks[o], wk[k][:, o * 128:(o + 1) * 128], rhs,
                       start=(k == 0), stop=(k == PT - 1))
        for o in range(PT):
            t = ctx.scr_b(4 + o % 2)
            ctx.act(t, ctx.pbanks[o], AF.Relu)
            ctx.act(kact_c[o], t, AF.Square)
        for k in range(PT):
            rhs = br(k, c)
            for o in range(PT):
                ctx.mm(ctx.pbanks[o], wr[k][:, o * 128:(o + 1) * 128], rhs,
                       start=(k == 0), stop=(k == PT - 1))
        for o in range(PT):
            ctx.act(r_c[o], ctx.pbanks[o], AF.Sigmoid)
        for k in range(PT):
            for o in range(PT):
                ctx.mm(ctx.pbanks[o], wv[k][:, o * 128:(o + 1) * 128], kact_c[k],
                       start=(k == 0), stop=(k == PT - 1))
        for o in range(PT):
            vt = ctx.scr_b(4 + o % 2)
            ctx.tt(vt, r_c[o], ctx.pbanks[o], OP.mult)
            ctx.stt(x[o][:, sl], vt, scale, x[o][:, sl], OP.mult, OP.add)



I_FFN = 2816
IT = I_FFN // 128


def absmax_scalar(ctx, tiles, ident_f, ps_bank, out_row):
    cols = ctx.sb(128, len(tiles), F32)
    col = ctx.sb(128, 1, F32)
    for i, t in enumerate(tiles):
        ctx.red(cols[:, i:i + 1], t, OP.max, absval=True)
    ctx.red(col, cols, OP.max)
    ctx.transpose(ps_bank[0:1, 0:128], col, ident_f)
    ctx.red(out_row, ps_bank[0:1, 0:128], OP.max)


def quant_chunks(ctx, dst, src, inv_col, ncols, cw=TC):
    c0 = 0
    while c0 < ncols:
        w = min(cw, ncols - c0)
        sl = slice(c0, c0 + w)
        f = ctx.scr_f(0)
        i = ctx.scr_i(0)
        ctx.ts(f[:, 0:w], src[:, sl], inv_col, OP.mult)
        ctx.copy(i[:, 0:w], f[:, 0:w])
        ctx.ts(dst[:, sl], i[:, 0:w], -128.0, OP.max, 127.0, OP.min)
        c0 += w


def scalar_bcast(ctx, ps_bank, row11, out_col):
    ctx.mm(ps_bank[:, 0:1], ctx.ones_row_f32, row11)
    ctx.copy(out_col, ps_bank[:, 0:1])


def qt_weight(ctx, w_d, wqT_d, sc_d, R, C, ident_b):
    RT, CT = R // 128, C // 128
    wscr = [ctx.sb(128, C, F32) for _ in range(3)]
    wq = [ctx.sb(128, C, BF16) for _ in range(RT)]
    scol = ctx.sb(128, RT, F32)
    mx = ctx.sb(128, 1, F32)
    inv = ctx.sb(128, 1, F32)
    out_sb = ctx.sb(128, R, BF16)
    psT = ctx.ps(128, 128, BF16)
    for rt in range(RT):
        s = wscr[rt % 3]
        ctx.dma(s, w_d[rt * 128:(rt + 1) * 128, :])
        ctx.red(mx, s, OP.max, absval=True)
        ctx.ts(scol[:, rt:rt + 1], mx, 1.0 / 127.0, OP.mult)
        ctx.recip(inv, scol[:, rt:rt + 1])
        quant_chunks(ctx, wq[rt], s, inv, C)
    for j in range(CT):
        for rt in range(RT):
            ctx.transpose(psT, wq[rt][:, j * 128:(j + 1) * 128], ident_b)
            ctx.copy(out_sb[:, rt * 128:(rt + 1) * 128], psT)
        ctx.dma(wqT_d[j * 128:(j + 1) * 128, :], out_sb)
    ctx.dma(sc_d, scol)


def body_ffn0_prep(ctx, ins, outs):
    setup_consts(ctx, ins)
    ident_b = ctx.sb(128, 128, BF16)
    ctx.dma(ident_b, ins["c_eyeb"])
    qt_weight(ctx, ins["w_gate"], outs["wgT"], outs["sg"], I_FFN, D, ident_b)
    qt_weight(ctx, ins["w_up"], outs["wuT"], outs["su"], I_FFN, D, ident_b)
    qt_weight(ctx, ins["w_down"], outs["wdT"], outs["sdn"], D, I_FFN, ident_b)


def body_ffn0_a(ctx, ins, outs):
    ones_col = setup_consts(ctx, ins)
    ident_f = ctx.sb(128, 128, F32)
    ctx.dma(ident_f, ins["c_eyef"])
    x = [ctx.sb(128, T, F32) for _ in range(PT)]
    for p in range(PT):
        ctx.dma(x[p], ins["x"][p * 128:(p + 1) * 128, :])
    xn = layernorm(ctx, x, ones_col, out_dt=F32, out_tiles=x)[0]
    lmax = ctx.sb(1, 1, F32)
    absmax_scalar(ctx, xn, ident_f, ctx.pbanks[4], lmax)
    for p in range(PT):
        ctx.dma(outs["xn"][p * 128:(p + 1) * 128, :], xn[p])
    ctx.dma(outs["lmax"], lmax)


def body_ffn0_b(ctx, ins, outs):
    setup_consts(ctx, ins)
    ident_f = ctx.sb(128, 128, F32)
    ctx.dma(ident_f, ins["c_eyef"])
    xnst = [ctx.sb(128, T, F32) for _ in range(4)]
    xq = [ctx.sb(128, T, BF16) for _ in range(PT)]
    wg = [ctx.sb(128, I_FFN, BF16) for _ in range(PT)]
    wu = [ctx.sb(128, I_FFN, BF16) for _ in range(PT)]
    for k in range(PT):
        ctx.dma(wg[k], ins["wgT"][k * 128:(k + 1) * 128, :])
        ctx.dma(wu[k], ins["wuT"][k * 128:(k + 1) * 128, :])
    sg = ctx.sb(128, IT, F32)
    su = ctx.sb(128, IT, F32)
    ctx.dma(sg, ins["sg"])
    ctx.dma(su, ins["su"])

    g8 = ctx.sb(1, NCORE, F32)
    clip_sb = ctx.sb(1, 1, F32)
    gmax = ctx.sb(1, 1, F32)
    sx = ctx.sb(1, 1, F32)
    inv_sx = ctx.sb(1, 1, F32)
    ctx.dma(g8, ins["gmax8"])
    ctx.dma(clip_sb, ins["clip_x"])
    ctx.red(gmax, g8, OP.max)
    ctx.act(sx, clip_sb, AF.Sigmoid)
    ctx.tt(sx, sx, gmax, OP.mult)
    ctx.ts(sx, sx, 1.0 / 127.0, OP.mult)
    ctx.recip(inv_sx, sx)
    invb = ctx.sb(128, 1, F32)
    sxb = ctx.sb(128, 1, F32)
    scalar_bcast(ctx, ctx.pbanks[4], inv_sx, invb)
    scalar_bcast(ctx, ctx.pbanks[4], sx, sxb)
    sxg = ctx.sb(128, IT, F32)
    sxu = ctx.sb(128, IT, F32)
    ctx.ts(sxg, sg, sxb, OP.mult)
    ctx.ts(sxu, su, sxb, OP.mult)

    for p in range(PT):
        st = xnst[p % 4]
        ctx.dma(st, ins["xn"][p * 128:(p + 1) * 128, :])
        quant_chunks(ctx, xq[p], st, invb, T)

    hmax_cols = ctx.sb(128, IT * NTC, F32)
    for o in range(IT):
        for c in range(NTC):
            sl = slice(c * TC, (c + 1) * TC)
            psg = ctx.pbanks[2 * (c % 2)]
            psu = ctx.pbanks[2 * (c % 2) + 1]
            e1 = ctx.scr_f(c % 2)
            e2 = ctx.scr_f(2 + c % 2)
            for k in range(4):
                ctx.mm(psg, wg[k][:, o * 128:(o + 1) * 128], xq[k][:, sl],
                       start=(k == 0), stop=(k == 3))
            ctx.copy(e1, psg, eng="scalar")
            for k in range(4, PT):
                ctx.mm(psg, wg[k][:, o * 128:(o + 1) * 128], xq[k][:, sl],
                       start=(k == 4), stop=(k == PT - 1))
            ctx.tt(e1, e1, psg, OP.add)
            for k in range(4):
                ctx.mm(psu, wu[k][:, o * 128:(o + 1) * 128], xq[k][:, sl],
                       start=(k == 0), stop=(k == 3))
            ctx.copy(e2, psu, eng="scalar")
            for k in range(4, PT):
                ctx.mm(psu, wu[k][:, o * 128:(o + 1) * 128], xq[k][:, sl],
                       start=(k == 4), stop=(k == PT - 1))
            ctx.tt(e2, e2, psu, OP.add)
            gs = ctx.scr_f(4 + c % 2)
            ht = ctx.scr_f(6 + (o * NTC + c) % 4)
            ctx.act(gs, e1, AF.Silu, scale=sxg[:, o:o + 1])
            ctx.ts(e2, e2, sxu[:, o:o + 1], OP.mult)
            ctx.tt(ht, gs, e2, OP.mult)
            ctx.red(hmax_cols[:, o * NTC + c:o * NTC + c + 1], ht, OP.max, absval=True)
            ctx.dma(outs["h"][o * 128:(o + 1) * 128, sl], ht)
    col = ctx.sb(128, 1, F32)
    lmax = ctx.sb(1, 1, F32)
    ctx.red(col, hmax_cols, OP.max)
    ctx.transpose(ctx.pbanks[4][0:1, 0:128], col, ident_f)
    ctx.red(lmax, ctx.pbanks[4][0:1, 0:128], OP.max)
    ctx.dma(outs["hmax"], lmax)


def body_ffn0_c(ctx, ins, outs):
    setup_consts(ctx, ins)
    sdn = ctx.sb(128, PT, F32)
    ctx.dma(sdn, ins["sdn"])

    g8 = ctx.sb(1, NCORE, F32)
    clip_sb = ctx.sb(1, 1, F32)
    gmax = ctx.sb(1, 1, F32)
    sh = ctx.sb(1, 1, F32)
    inv_sh = ctx.sb(1, 1, F32)
    ctx.dma(g8, ins["hmax8"])
    ctx.dma(clip_sb, ins["clip_down"])
    ctx.red(gmax, g8, OP.max)
    ctx.act(sh, clip_sb, AF.Sigmoid)
    ctx.tt(sh, sh, gmax, OP.mult)
    ctx.ts(sh, sh, 1.0 / 127.0, OP.mult)
    ctx.recip(inv_sh, sh)
    invb = ctx.sb(128, 1, F32)
    shb = ctx.sb(128, 1, F32)
    scalar_bcast(ctx, ctx.pbanks[0], inv_sh, invb)
    scalar_bcast(ctx, ctx.pbanks[0], sh, shb)
    eff = ctx.sb(128, PT, F32)
    ctx.ts(eff, sdn, shb, OP.mult, 0.5, OP.mult)

    wd = [ctx.sb(128, D, BF16) for _ in range(IT)]
    hq = [ctx.sb(128, T, BF16) for _ in range(IT)]
    hst = [ctx.sb(128, T, F32) for _ in range(3)]
    for k in range(IT):
        ctx.dma(wd[k], ins["wdT"][k * 128:(k + 1) * 128, :])
        ctx.dma(hst[k % 3], ins["h"][k * 128:(k + 1) * 128, :])
        quant_chunks(ctx, hq[k], hst[k % 3], invb, T)

    xst = [ctx.sb(128, TC, F32) for _ in range(4)]
    yst = [ctx.sb(128, TC, F32) for _ in range(4)]
    for c in range(NTC):
        sl = slice(c * TC, (c + 1) * TC)
        for k in range(IT):
            for o in range(PT):
                ctx.mm(ctx.pbanks[o], wd[k][:, o * 128:(o + 1) * 128], hq[k][:, sl],
                       start=(k == 0), stop=(k == IT - 1))
        for o in range(PT):
            t = ctx.scr_b(o % 2)
            xc = xst[o % 4]
            yo = yst[o % 4]
            ctx.act(t, ctx.pbanks[o], AF.Copy, scale=eff[:, o:o + 1], bias=0.0)
            ctx.dma(xc, ins["x"][o * 128:(o + 1) * 128, sl])
            ctx.tt(yo, xc, t, OP.add)
            ctx.dma(outs["y"][o * 128:(o + 1) * 128, sl], yo)



INNER = 2048
CTI = INNER // 128


def exp_silu(ctx, out, in_t, sA, sB):
    ctx.act(sA, in_t, AF.Exp, scale=-1.0)
    ctx.ts(sA, sA, 1.0, OP.add)
    ctx.recip(sB, sA)
    ctx.tt(out, in_t, sB, OP.mult)


def body_mamba(ctx, ins, outs):
    ones_col = setup_consts(ctx, ins)
    xd = ins["x"]

    xbuf = [ctx.sb(128, TC, F32) for _ in range(4)]

    def xprov(p, c, bi):
        b = xbuf[bi % 4]
        ctx.dma(b, xd[p * 128:(p + 1) * 128, c * TC:(c + 1) * TC])
        return b

    xn = layernorm_str(ctx, xprov, ones_col)

    cw = ctx.sb(128, CTI * 4, F32)
    cb = ctx.sb(128, CTI, F32)
    Dp = ctx.sb(128, CTI, F32)
    alog = ctx.sb(128, CTI * 16, F32)
    A_sb = ctx.sb(128, CTI * 16, F32)
    ctx.dma(cw, ins["conv_w"])
    ctx.dma(cb, ins["conv_b"])
    ctx.dma(Dp, ins["D"])
    ctx.dma(alog, ins["A_log"])
    ctx.act(A_sb, alog, AF.Exp)
    ctx.ts(A_sb, A_sb, -1.0, OP.mult)

    wsl = [[ctx.sb(128, 256, BF16) for _ in range(PT)] for _ in range(2)]
    y_full = [ctx.sb(128, T, BF16) for _ in range(CTI)]

    xssm = ctx.sb(128, T + 3, BF16)
    xconv = ctx.sb(128, T, BF16)
    delta = ctx.sb(128, T, BF16)
    Bx = ctx.sb(128, T, BF16)
    a_d = ctx.sb(128, T, BF16)
    h_t = ctx.sb(128, T, BF16)
    ysum = ctx.sb(128, T, BF16)
    res_t = ctx.sb(128, T, BF16)
    sres = ctx.sb(128, T, BF16)
    t_a = ctx.sb(128, T, BF16)
    t_b = ctx.sb(128, T, BF16)
    s_f1 = ctx.sb(128, T, F32)
    s_f2 = ctx.sb(128, T, F32)

    ctx.memset(xssm[:, 0:3], 0.0)
    WinT = ins["W_inT"]

    for i in range(CTI):
        ws = wsl[i % 2]
        for k in range(PT):
            ctx.dma(ws[k][:, 0:128], WinT[k * 128:(k + 1) * 128, i * 128:(i + 1) * 128], eng="gpsimd")
            ctx.dma(ws[k][:, 128:256], WinT[k * 128:(k + 1) * 128, INNER + i * 128:INNER + (i + 1) * 128], eng="gpsimd")
        for c in range(NTC):
            sl = slice(c * TC, (c + 1) * TC)
            psA = ctx.pbanks[2 * (c % 2)]
            psB = ctx.pbanks[2 * (c % 2) + 1]
            for k in range(PT):
                ctx.mm(psA, ws[k][:, 0:128], xn[k][:, sl], start=(k == 0), stop=(k == PT - 1))
            for k in range(PT):
                ctx.mm(psB, ws[k][:, 128:256], xn[k][:, sl], start=(k == 0), stop=(k == PT - 1))
            ctx.cast(xssm[:, 3 + c * TC:3 + (c + 1) * TC], psA)
            ctx.cast(res_t[:, sl], psB)
        ctx.ts(t_a, xssm[:, 0:T], cw[:, 4 * i:4 * i + 1], OP.mult)
        for j in range(1, 4):
            ctx.stt(t_a, xssm[:, j:j + T], cw[:, 4 * i + j:4 * i + j + 1], t_a, OP.mult, OP.add)
        ctx.ts(t_b, t_a, cb[:, i:i + 1], OP.add)
        exp_silu(ctx, xconv, t_b, s_f1, s_f2)
        ctx.act(t_a, xconv, AF.Exp)
        ctx.act(delta, t_a, AF.Ln, bias=1.0)
        ctx.act(Bx, xconv, AF.Square)
        for n in range(16):
            ctx.act(a_d, delta, AF.Exp, scale=A_sb[:, i * 16 + n:i * 16 + n + 1])
            ctx.scan(h_t, a_d, Bx, 0.0)
            if n == 0:
                ctx.copy(ysum, h_t)
            else:
                ctx.tt(ysum, ysum, h_t, OP.add)
        exp_silu(ctx, sres, res_t, s_f1, s_f2)
        ctx.tt(h_t, xconv, ysum, OP.mult)
        ctx.tt(h_t, h_t, sres, OP.mult)
        ctx.stt(y_full[i], xconv, Dp[:, i:i + 1], h_t, OP.mult, OP.add)

    WoT = ins["W_outT"]
    wo = [ctx.sb(128, D, BF16) for _ in range(4)]
    for c in range(NTC):
        sl = slice(c * TC, (c + 1) * TC)
        for k in range(CTI):
            ctx.dma(wo[k % 4], WoT[k * 128:(k + 1) * 128, :], eng="gpsimd")
            for o in range(PT):
                ctx.mm(ctx.pbanks[o], wo[k % 4][:, o * 128:(o + 1) * 128], y_full[k][:, sl],
                       start=(k == 0), stop=(k == CTI - 1))
        for o in range(PT):
            xc = xbuf[o % 4]
            ctx.dma(xc, xd[o * 128:(o + 1) * 128, sl])
            yo = ctx.scr_f(o % 4)
            ctx.tt(yo, xc, ctx.pbanks[o], OP.add)
            ctx.dma(outs["y"][o * 128:(o + 1) * 128, sl], yo)


def layernorm_str(ctx, xprov, ones_col, eps=1e-5, out_dt=BF16, out_tiles=None):
    ps_s, ps_q, ps_b, ps_r = ctx.pbanks[0], ctx.pbanks[1], ctx.pbanks[2], ctx.pbanks[3]
    sq = ctx.scr_f(0)
    if not hasattr(ctx, "_ln_rows_c"):
        ctx._ln_rows_c = [ctx.sb(1, TC, F32), ctx.sb(1, TC, F32),
                          ctx.sb(1, TC, F32), ctx.sb(1, TC, F32)]
    m_row, rstd_row, v_row, tmp_row = ctx._ln_rows_c
    xn = out_tiles if out_tiles is not None else [ctx.sb(128, T, out_dt) for _ in range(PT)]
    for c in range(NTC):
        sl = slice(c * TC, (c + 1) * TC)
        for p in range(PT):
            xt = xprov(p, c, p)
            ctx.mm(ps_s[0:1, :], ones_col, xt, start=(p == 0), stop=(p == PT - 1))
            ctx.act(sq, xt, AF.Square)
            ctx.mm(ps_q[0:1, :], ones_col, sq, start=(p == 0), stop=(p == PT - 1))
        ctx.copy(m_row, ps_s[0:1, :])
        ctx.act(tmp_row, m_row, AF.Square)
        ctx.tt(v_row, ps_q[0:1, :], tmp_row, OP.subtract)
        ctx.ts(v_row, v_row, 0.0, OP.max)
        ctx.act(tmp_row, v_row, AF.Sqrt, bias=eps)
        ctx.recip(rstd_row, tmp_row)
        ctx.mm(ps_b, ctx.ones_row_f32, m_row)
        ctx.mm(ps_r, ctx.ones_row_f32, rstd_row)
        for p in range(PT):
            xt = xprov(p, c, p)
            t1 = ctx.scr_f(0)
            ctx.tt(t1, xt, ps_b[:, :], OP.subtract)
            ctx.tt(xn[p][:, sl], t1, ps_r[:, :], OP.mult)
    return xn



NH = 16
HD = 64
NTCH = T // 128


def body_l1attn_a(ctx, ins, outs):
    ones_col = setup_consts(ctx, ins)
    xd = ins["x"]
    xbuf = [ctx.sb(128, TC, F32) for _ in range(4)]

    def xprov(p, c, bi):
        b = xbuf[bi % 4]
        ctx.dma(b, xd[p * 128:(p + 1) * 128, c * TC:(c + 1) * TC])
        return b

    xn = layernorm_str(ctx, xprov, ones_col)

    qw = [ctx.sb(128, D, BF16) for _ in range(PT)]
    load_cast_bf16(ctx, ins["qw"], D, D, out_tiles=qw)

    q = [ctx.sb(128, T, BF16) for _ in range(PT)]
    for c in range(NTC):
        sl = slice(c * TC, (c + 1) * TC)
        for j in range(PT):
            ps = ctx.pbanks[4 + j % 2]
            for k in range(PT):
                ctx.mm(ps, qw[k][:, j * 128:(j + 1) * 128], xn[k][:, sl],
                       start=(k == 0), stop=(k == PT - 1))
            ctx.cast(q[j][:, sl], ps)
    for p in range(PT):
        ctx.dma(outs["q"][p * 128:(p + 1) * 128, :], q[p])

    hpat = ctx.sb(128, NH * PT, BF16)
    ctx.dma(hpat, ins["hpat"])
    psK = ctx.pbanks[6]
    sq = [ctx.sb(128, T, BF16) for _ in range(PT)]
    for j in range(PT):
        ctx.act(sq[j], q[j], AF.Square)
    ksq = ctx.sb(128, NTCH * NH, F32)
    for tc in range(NTCH):
        for j in range(PT):
            ctx.mm(psK[:, 0:NH], sq[j][:, tc * 128:(tc + 1) * 128],
                   hpat[:, j * NH:(j + 1) * NH], start=(j == 0), stop=(j == PT - 1))
        ctx.copy(ksq[:, tc * NH:(tc + 1) * NH], psK[:, 0:NH])
    ctx.dma(outs["ksq"], ksq)


def body_l1attn_b(ctx, ins, outs):
    setup_consts(ctx, ins)
    hpat = ctx.sb(128, NH * PT, BF16)
    ctx.dma(hpat, ins["hpat"])
    q_me = [ctx.sb(128, T, BF16) for _ in range(PT)]
    for p in range(PT):
        ctx.dma(q_me[p], ins["q_me"][p * 128:(p + 1) * 128, :])
    ksq_sb = [ctx.sb(128, NTCH * NH, F32) for _ in range(NCORE)]
    for s in range(NCORE):
        ctx.dma(ksq_sb[s], ins["ksq_all"][s * 128:(s + 1) * 128, :])

    qs = [ctx.sb(128, T, BF16) for _ in range(PT)]
    prod = [ctx.sb(128, T, BF16) for _ in range(PT)]
    Racc = [ctx.sb(128, NTCH * NH, F32) for _ in range(NCORE)]
    psR = ctx.pbanks[0]
    t256 = ctx.sb(128, NTCH * NH, F32)
    for s in range(NCORE):
        for j in range(PT):
            ctx.dma(qs[j], ins["q_all"][s * D + j * 128:s * D + (j + 1) * 128, :])
            ctx.tt(prod[j], q_me[j], qs[j], OP.mult)
        for tc in range(NTCH):
            for j in range(PT):
                ctx.mm(psR[:, 0:NH], prod[j][:, tc * 128:(tc + 1) * 128],
                       hpat[:, j * NH:(j + 1) * NH], start=(j == 0), stop=(j == PT - 1))
            cs = slice(tc * NH, (tc + 1) * NH)
            ctx.ts(t256[:, cs], psR[:, 0:NH], 0.25, OP.mult)
            ctx.stt(Racc[s][:, cs], ksq_sb[s][:, cs], -0.125, t256[:, cs], OP.mult, OP.add)

    ctx.barrier()
    mxp = [ctx.sb(128, NTCH * NH, F32) for _ in range(2)]
    smp = [ctx.sb(128, NTCH * NH, F32) for _ in range(2)]
    rn = ctx.sb(128, NTCH * NH, F32)
    E = [ctx.sb(128, NTCH * NH, BF16) for _ in range(NCORE)]
    P = [ctx.sb(128, NTCH * NH, BF16) for _ in range(NCORE)]
    ctx.tt(mxp[0], Racc[0], Racc[1], OP.max)
    for s in range(2, NCORE):
        ctx.tt(mxp[(s - 1) % 2], mxp[s % 2], Racc[s], OP.max)
    mx = mxp[(NCORE - 2) % 2]
    for s in range(NCORE):
        ctx.tt(t256, Racc[s], mx, OP.subtract)
        ctx.act(E[s], t256, AF.Exp)
    ctx.tt(smp[0], E[0], E[1], OP.add)
    for s in range(2, NCORE):
        ctx.tt(smp[(s - 1) % 2], smp[s % 2], E[s], OP.add)
    ctx.recip(rn, smp[(NCORE - 2) % 2])
    for s in range(NCORE):
        ctx.tt(P[s], E[s], rn, OP.mult)

    ctx.barrier()
    qw = [prod[k][:, 0:D] for k in range(PT)]
    vw = [prod[k][:, D:2 * D] for k in range(PT)]
    for k in range(PT):
        ctx.dma(qw[k], ins["qw"][k * 128:(k + 1) * 128, :], eng="gpsimd")
        ctx.dma(vw[k], ins["vw"][k * 128:(k + 1) * 128, :], eng="gpsimd")
    psM = ctx.pbanks[1]
    M_sb = ctx.sb(64, NH * HD, F32)
    for h in range(NH):
        for k in range(PT):
            ctx.mm(psM[0:HD, 0:HD], qw[k][:, h * HD:(h + 1) * HD],
                   vw[k][:, h * HD:(h + 1) * HD], start=(k == 0), stop=(k == PT - 1))
        ctx.ts(M_sb[0:HD, h * HD:(h + 1) * HD], psM[0:HD, 0:HD], 0.125, OP.mult)
    BD = [ctx.sb(128, 128, BF16) for _ in range(PT)]
    for j in range(PT):
        ctx.memset(BD[j], 0.0)
        h0, h1 = 2 * j, 2 * j + 1
        ctx.dma(BD[j][0:HD, 0:HD], M_sb[0:HD, h0 * HD:(h0 + 1) * HD], eng="gpsimd")
        ctx.dma(BD[j][HD:128, HD:128], M_sb[0:HD, h1 * HD:(h1 + 1) * HD], eng="gpsimd")

    ctx.barrier()
    acc = [ctx.sb(128, D, BF16) for _ in range(NTCH)]
    u_sb = ctx.sb(128, D, BF16)
    tmp = ctx.sb(128, D, BF16)
    for s in range(NCORE):
        for j in range(PT):
            ctx.dma(qs[j], ins["q_all"][s * D + j * 128:s * D + (j + 1) * 128, :])
        for tc in range(NTCH):
            for j in range(PT):
                bank = ctx.pbanks[3 + (j // 4) % 2]
                ctx.mm(bank[:, (j % 4) * 128:(j % 4 + 1) * 128],
                       qs[j][:, tc * 128:(tc + 1) * 128], BD[j],
                       start=True, stop=True)
            ctx.cast(u_sb[:, 0:TC], ctx.pbanks[3])
            ctx.cast(u_sb[:, TC:D], ctx.pbanks[4])
            pb = Tile(P[s].ap[:, tc * NH:(tc + 1) * NH]
                      .rearrange("q (h o) -> q h o", o=1).broadcast_to([128, NH, HD]),
                      P[s].root)
            ctx.tt(Tile(tmp.ap.rearrange("q (h o) -> q h o", o=HD), tmp.root),
                   Tile(u_sb.ap.rearrange("q (h o) -> q h o", o=HD), u_sb.root),
                   pb, OP.mult)
            if s == 0:
                ctx.copy(acc[tc], tmp)
            else:
                ctx.tt(acc[tc], acc[tc], tmp, OP.add)

    ctx.barrier()
    ident_b = ctx.sb(128, 128, BF16)
    ctx.dma(ident_b, ins["c_eyeb"])
    ow = [q_me[k][:, 0:D] for k in range(PT)]
    load_cast_bf16(ctx, ins["out_wT"], D, D, out_tiles=ow)
    ob = ctx.sb(128, PT, F32)
    ctx.dma(ob, ins["out_b"])
    pxv = [qs[j][:, 0:TC] for j in range(PT)]
    psT = ctx.ps(128, 128, BF16)
    xcp = [ctx.sb(128, TC, F32) for _ in range(4)]
    yop = [ctx.sb(128, TC, F32) for _ in range(4)]
    for c in range(NTC):
        sl = slice(c * TC, (c + 1) * TC)
        for tci in range(4):
            tc = c * 4 + tci
            for j in range(PT):
                ctx.transpose(psT, acc[tc][:, j * 128:(j + 1) * 128], ident_b)
                ctx.copy(pxv[j][:, tci * 128:(tci + 1) * 128], psT)
        for o in range(PT):
            ps = ctx.pbanks[5 + o % 2]
            for k in range(PT):
                ctx.mm(ps, ow[k][:, o * 128:(o + 1) * 128], pxv[k],
                       start=(k == 0), stop=(k == PT - 1))
            t = ctx.scr_b(o % 2)
            xc = xcp[o % 4]
            yo = yop[o % 4]
            ctx.ts(t, ps, ob[:, o:o + 1], OP.add)
            ctx.dma(xc, ins["x"][o * 128:(o + 1) * 128, sl])
            ctx.tt(yo, xc, t, OP.add)
            ctx.dma(outs["y"][o * 128:(o + 1) * 128, sl], yo)


def run_l1attn(x_fm_cores, q_weight, v_weight, out_w, out_b):
    qw2d = np.ascontiguousarray(q_weight.reshape(D, D))
    vw2d = np.ascontiguousarray(v_weight.reshape(D, D))
    hpat = np.zeros((128, NH * PT), np.float32)
    for j in range(PT):
        hpat[0:HD, j * NH + 2 * j] = 1.0
        hpat[HD:128, j * NH + 2 * j + 1] = 1.0
    hpat = hpat.astype(BF)

    spec_in = {"x": ((D, T), F32), "qw": ((D, D), F32), "hpat": ((128, NH * PT), BF16),
               **_spec_consts()}
    spec_out = {"q": ((D, T), BF16), "ksq": ((128, NTCH * NH), F32)}
    nc = build_program(body_l1attn_a, spec_in, spec_out)
    shared = {"qw": qw2d, "hpat": hpat, **CONST_INPUTS}
    res = run_bass_kernel_spmd(nc, [{"x": xc, **shared} for xc in x_fm_cores],
                               core_ids=list(range(NCORE)))
    q_cores = [r["q"] for r in res.results]
    ksq_all = np.concatenate([r["ksq"] for r in res.results], axis=0)
    q_all = np.concatenate(q_cores, axis=0)

    spec_in = {"x": ((D, T), F32), "q_me": ((D, T), BF16),
               "q_all": ((NCORE * D, T), BF16), "ksq_all": ((NCORE * 128, NTCH * NH), F32),
               "qw": ((D, D), F32), "vw": ((D, D), F32), "out_wT": ((D, D), F32),
               "out_b": ((128, PT), F32), "hpat": ((128, NH * PT), BF16),
               "c_eyeb": ((128, 128), BF16), **_spec_consts()}
    spec_out = {"y": ((D, T), F32)}
    nc = build_program(body_l1attn_b, spec_in, spec_out, n_banks=7)
    shared = {"q_all": q_all, "ksq_all": ksq_all, "qw": qw2d, "vw": vw2d,
              "out_wT": np.ascontiguousarray(out_w.T), "out_b": _ppn(out_b, PT),
              "hpat": hpat, "c_eyeb": CONST_INPUTS_EYE["c_eyeb"], **CONST_INPUTS}
    res = run_bass_kernel_spmd(
        nc, [{"x": xc, "q_me": qc, **shared} for xc, qc in zip(x_fm_cores, q_cores)],
        core_ids=list(range(NCORE)))
    return [r["y"] for r in res.results]



def _fm(a):
    return np.ascontiguousarray(a.T)


def _pp(v):
    return np.ascontiguousarray(v.reshape(PT, 128).T)


def run_l1ffn(x_fm_cores, W_r, W_k, W_v, mu_r, mu_k):
    spec_in = {
        "x": ((D, T), F32), "W_rT": ((D, D), F32), "W_kT": ((D, D), F32),
        "W_vT": ((D, D), F32), "mu_r": ((128, PT), F32), "mu_k": ((128, PT), F32),
        "c_ones128": ((1, 128), F32), "c_onesD": ((128, 1), F32),
    }
    spec_out = {"y": ((D, T), F32)}
    nc = build_program(body_l1ffn, spec_in, spec_out)
    shared = {
        "W_rT": np.ascontiguousarray(W_r.T), "W_kT": np.ascontiguousarray(W_k.T),
        "W_vT": np.ascontiguousarray(W_v.T), "mu_r": _pp(mu_r), "mu_k": _pp(mu_k),
        **CONST_INPUTS,
    }
    in_maps = [{"x": xc, **shared} for xc in x_fm_cores]
    res = run_bass_kernel_spmd(nc, in_maps, core_ids=list(range(NCORE)))
    return [r["y"] for r in res.results]


def run_mamba(x_fm_cores, W_in, conv_w, conv_b, A_log, Dv, W_out):
    spec_in = {
        "x": ((D, T), F32), "W_inT": ((D, 2 * INNER), F32),
        "W_outT": ((INNER, D), F32), "conv_w": ((128, CTI * 4), F32),
        "conv_b": ((128, CTI), F32), "D": ((128, CTI), F32),
        "A_log": ((128, CTI * 16), F32), **_spec_consts(),
    }
    spec_out = {"y": ((D, T), F32)}
    nc = build_program(body_mamba, spec_in, spec_out)
    shared = {
        "W_inT": np.ascontiguousarray(W_in.T),
        "W_outT": np.ascontiguousarray(W_out.T),
        "conv_w": np.ascontiguousarray(conv_w.reshape(CTI, 128, 4).transpose(1, 0, 2).reshape(128, CTI * 4)),
        "conv_b": _ppn(conv_b, CTI), "D": _ppn(Dv, CTI),
        "A_log": np.ascontiguousarray(A_log.reshape(CTI, 128, 16).transpose(1, 0, 2).reshape(128, CTI * 16)),
        **CONST_INPUTS,
    }
    in_maps = [{"x": xc, **shared} for xc in x_fm_cores]
    res = run_bass_kernel_spmd(nc, in_maps, core_ids=list(range(NCORE)))
    return [r["y"] for r in res.results]


def _ppn(v, n):
    return np.ascontiguousarray(v.reshape(n, 128).T)


CONST_INPUTS_EYE = {
    "c_eyef": np.eye(128, dtype=np.float32),
    "c_eyeb": np.eye(128, dtype=BF),
}


def _spec_consts():
    return {"c_ones128": ((1, 128), F32), "c_onesD": ((128, 1), F32)}


def run_ffn0_prep(w_gate, w_up, w_down):
    spec_in = {"w_gate": ((I_FFN, D), F32), "w_up": ((I_FFN, D), F32),
               "w_down": ((D, I_FFN), F32), "c_eyeb": ((128, 128), BF16),
               **_spec_consts()}
    spec_out = {"wgT": ((D, I_FFN), BF16), "wuT": ((D, I_FFN), BF16),
                "wdT": ((I_FFN, D), BF16),
                "sg": ((128, IT), F32), "su": ((128, IT), F32), "sdn": ((128, PT), F32)}
    nc = build_program(body_ffn0_prep, spec_in, spec_out, n_banks=4)
    shared = {"w_gate": w_gate, "w_up": w_up, "w_down": w_down,
              "c_eyeb": CONST_INPUTS_EYE["c_eyeb"], **CONST_INPUTS}
    res = run_bass_kernel_spmd(nc, [dict(shared) for _ in range(NCORE)],
                               core_ids=list(range(NCORE)))
    return res.results[0]


def run_ffn0(x_fm_cores, prep, clip_x, clip_down):
    spec_in = {"x": ((D, T), F32), "c_eyef": ((128, 128), F32), **_spec_consts()}
    spec_out = {"xn": ((D, T), F32), "lmax": ((1, 1), F32)}
    nc = build_program(body_ffn0_a, spec_in, spec_out)
    shared = {"c_eyef": CONST_INPUTS_EYE["c_eyef"], **CONST_INPUTS}
    res = run_bass_kernel_spmd(nc, [{"x": xc, **shared} for xc in x_fm_cores],
                               core_ids=list(range(NCORE)))
    xns = [r["xn"] for r in res.results]
    gmax8 = np.concatenate([r["lmax"] for r in res.results], axis=1)

    spec_in = {"xn": ((D, T), F32), "gmax8": ((1, NCORE), F32),
               "clip_x": ((1, 1), F32), "wgT": ((D, I_FFN), BF16),
               "wuT": ((D, I_FFN), BF16), "sg": ((128, IT), F32),
               "su": ((128, IT), F32), "c_eyef": ((128, 128), F32), **_spec_consts()}
    spec_out = {"h": ((I_FFN, T), F32), "hmax": ((1, 1), F32)}
    nc = build_program(body_ffn0_b, spec_in, spec_out)
    shared = {"gmax8": gmax8, "clip_x": clip_x.reshape(1, 1),
              "wgT": prep["wgT"], "wuT": prep["wuT"], "sg": prep["sg"],
              "su": prep["su"], "c_eyef": CONST_INPUTS_EYE["c_eyef"], **CONST_INPUTS}
    res = run_bass_kernel_spmd(nc, [{"xn": xn, **shared} for xn in xns],
                               core_ids=list(range(NCORE)))
    hs = [r["h"] for r in res.results]
    hmax8 = np.concatenate([r["hmax"] for r in res.results], axis=1)

    spec_in = {"x": ((D, T), F32), "h": ((I_FFN, T), F32),
               "hmax8": ((1, NCORE), F32), "clip_down": ((1, 1), F32),
               "wdT": ((I_FFN, D), BF16), "sdn": ((128, PT), F32), **_spec_consts()}
    spec_out = {"y": ((D, T), F32)}
    nc = build_program(body_ffn0_c, spec_in, spec_out)
    shared = {"hmax8": hmax8, "clip_down": clip_down.reshape(1, 1),
              "wdT": prep["wdT"], "sdn": prep["sdn"], **CONST_INPUTS}
    res = run_bass_kernel_spmd(
        nc, [{"x": xc, "h": h, **shared} for xc, h in zip(x_fm_cores, hs)],
        core_ids=list(range(NCORE)))
    return [r["y"] for r in res.results]


def kernel(**inputs):
    x = np.ascontiguousarray(inputs["L0_attn_x"], dtype=np.float32)
    x_fm = [_fm(x[b]) for b in range(NCORE)]

    prep = run_ffn0_prep(np.ascontiguousarray(inputs["L0_ffn_w_gate"], dtype=np.float32),
                         np.ascontiguousarray(inputs["L0_ffn_w_up"], dtype=np.float32),
                         np.ascontiguousarray(inputs["L0_ffn_w_down"], dtype=np.float32))
    clip_x = np.asarray(inputs["L0_ffn_clip_x"], dtype=np.float32)
    clip_down = np.asarray(inputs["L0_ffn_clip_down"], dtype=np.float32)

    x_fm = run_ffn0(x_fm, prep, clip_x, clip_down)
    x_fm = run_mamba(x_fm,
                     np.ascontiguousarray(inputs["L0_attn_W_in"], dtype=np.float32),
                     np.ascontiguousarray(inputs["L0_attn_conv_w"], dtype=np.float32),
                     np.ascontiguousarray(inputs["L0_attn_conv_b"], dtype=np.float32),
                     np.ascontiguousarray(inputs["L0_attn_A_log"], dtype=np.float32),
                     np.ascontiguousarray(inputs["L0_attn_D"], dtype=np.float32),
                     np.ascontiguousarray(inputs["L0_attn_W_out"], dtype=np.float32))
    x_fm = run_ffn0(x_fm, prep, clip_x, clip_down)

    l1f = (np.ascontiguousarray(inputs["L1_ffn_W_r"], dtype=np.float32),
           np.ascontiguousarray(inputs["L1_ffn_W_k"], dtype=np.float32),
           np.ascontiguousarray(inputs["L1_ffn_W_v"], dtype=np.float32),
           np.ascontiguousarray(inputs["L1_ffn_mu_r"], dtype=np.float32),
           np.ascontiguousarray(inputs["L1_ffn_mu_k"], dtype=np.float32))
    x_fm = run_l1ffn(x_fm, *l1f)
    x_fm = run_l1attn(x_fm,
                      np.ascontiguousarray(inputs["L1_attn_q_weight"], dtype=np.float32),
                      np.ascontiguousarray(inputs["L1_attn_v_weight"], dtype=np.float32),
                      np.ascontiguousarray(inputs["L1_attn_out_w"], dtype=np.float32),
                      np.ascontiguousarray(inputs["L1_attn_out_b"], dtype=np.float32))
    x_fm = run_l1ffn(x_fm, *l1f)

    return np.stack([np.asarray(y, dtype=np.float32).T for y in x_fm], axis=0)


# version 05 (final)
# speedup vs baseline: 1.1126x; 1.1009x over previous
import numpy as np
import ml_dtypes
from contextlib import ExitStack
import concourse.bass as bass
import concourse.mybir as mybir
from concourse.alu_op_type import AluOpType as OP
from concourse.bass_utils import run_bass_kernel_spmd

F32 = mybir.dt.float32
BF16 = mybir.dt.bfloat16
I32 = mybir.dt.int32
AF = mybir.ActivationFunctionType
AX = mybir.AxisListType
BF = ml_dtypes.bfloat16

D = 1024
T = 2048
NCORE = 8
PT = D // 128
TC = 512
NTC = T // TC

ENGS = ("sync", "vector", "scalar", "tensor", "gpsimd")


class Tile:

    __slots__ = ("ap", "root", "writers", "readers")

    def __init__(self, ap, root=None):
        self.ap = ap
        self.root = root if root is not None else self
        if root is None:
            self.writers = {}
            self.readers = []

    def __getitem__(self, idx):
        return Tile(self.ap[idx], self.root)

    def rearrange(self, *a, **k):
        return Tile(self.ap.rearrange(*a, **k), self.root)


class _Op:
    __slots__ = ("eng", "fn", "deps", "idx", "dma", "need", "val", "waits")

    def __init__(self, eng, fn, deps, idx, dma):
        self.eng, self.fn, self.deps, self.idx, self.dma = eng, fn, deps, idx, dma
        self.need = False
        self.val = None
        self.waits = []


class Em:

    def __init__(self):
        self.ops = []
        self.per_eng = {e: [] for e in ENGS}

    def op(self, eng, fn, reads=(), writes=(), dma=False):
        deps = set()
        for t in reads:
            deps.update(t.root.writers.values())
        for t in writes:
            r = t.root
            deps.update(r.writers.values())
            deps.update(r.readers)
        o = _Op(eng, fn, deps, len(self.ops), dma)
        self.ops.append(o)
        self.per_eng[eng].append(o)
        for t in reads:
            t.root.readers.append(o.idx)
        for t in writes:
            r = t.root
            r.writers[(eng, dma)] = o.idx
            r.readers = []
        return o.idx

    def barrier_token(self):
        return len(self.ops)

    def wait_all(self, eng, token=None):
        tok = len(self.ops) if token is None else token
        deps = set(range(tok))
        o = _Op(eng, None, deps, len(self.ops), False)
        self.ops.append(o)
        self.per_eng[eng].append(o)
        return o.idx

    def finalize(self, serial=False):
        if serial:
            prev = None
            prev_kind = None
            for o in self.ops:
                if o.fn is None:
                    continue
                if o.dma:
                    kind = ("dma", o.eng)
                else:
                    kind = ("tensor",) if o.eng == "tensor" else None
                if prev is not None:
                    if kind is not None and kind == prev_kind:
                        o.deps = set()
                    else:
                        o.deps = set([prev])
                prev = o.idx
                prev_kind = kind
        for o in self.ops:
            for d in o.deps:
                p = self.ops[d]
                if p.fn is None:
                    continue
                if p.eng == o.eng == "tensor":
                    continue
                p.need = True
        val = {e: 0 for e in ENGS}
        dval = {e: 0 for e in ENGS}
        for o in self.ops:
            if o.fn is None:
                continue
            if o.dma:
                o.need = True
                dval[o.eng] += 16
                o.val = dval[o.eng]
            elif o.need:
                val[o.eng] += 1
                o.val = val[o.eng]
        self.totals = {}
        self.dtotals = {}
        for e in ENGS:
            last = None
            for o in reversed(self.per_eng[e]):
                if o.fn is not None and not o.dma:
                    last = o
                    break
            if last is not None and not last.need:
                last.need = True
                val[e] += 1
                last.val = val[e]
            self.totals[e] = val[e]
            self.dtotals[e] = dval[e]
        for e in ENGS:
            seen = {}
            for o in self.per_eng[e]:
                ws = {}
                for d in sorted(o.deps):
                    p = self.ops[d]
                    if p.fn is None or not p.need:
                        continue
                    if p.eng == o.eng == "tensor" and not p.dma and o.fn is not None and not o.dma:
                        continue
                    key = (p.eng, p.dma)
                    if p.val is None:
                        continue
                    ws[key] = max(ws.get(key, 0), p.val)
                for key, v in ws.items():
                    if seen.get(key, 0) >= v:
                        continue
                    seen[key] = v
                    o.waits.append((key, v))

    def emit(self, block, sems, dsems):
        def mk(e):
            def run(eng):
                for o in self.per_eng[e]:
                    for (pe, isdma), v in o.waits:
                        eng.wait_ge(dsems[pe] if isdma else sems[pe], v)
                    if o.fn is None:
                        continue
                    ins = o.fn(eng)
                    if o.need:
                        ins.then_inc(dsems[e] if o.dma else sems[e], 16 if o.dma else 1)
            return run

        for e in ENGS:
            if self.per_eng[e]:
                getattr(block, e)(mk(e))


class Ctx:

    def __init__(self, nc, st):
        self.nc = nc
        self.st = st
        self.em = Em()
        self._uid = 0
        self._scrf = []
        self._scrb = []
        self._scri = []

    def uid(self, p):
        self._uid += 1
        return f"{p}{self._uid}"

    def scr_f(self, i):
        while len(self._scrf) <= i:
            self._scrf.append(self.sb(128, TC, F32))
        return self._scrf[i]

    def scr_b(self, i):
        while len(self._scrb) <= i:
            self._scrb.append(self.sb(128, TC, BF16))
        return self._scrb[i]

    def barrier(self):
        tok = self.em.barrier_token()
        for e in ENGS:
            self.em.wait_all(e, tok)

    def scr_i(self, i):
        while len(self._scri) <= i:
            self._scri.append(self.sb(128, TC, I32))
        return self._scri[i]

    def sb(self, p, f, dt=F32, name=None):
        h = self.st.enter_context(self.nc.sbuf_tensor(name or self.uid("sb"), [p, f], dt))
        return Tile(h.ap())

    def ps(self, p=128, f=512, dt=F32):
        h = self.st.enter_context(self.nc.psum_tensor(self.uid("ps"), [p, f], dt))
        return Tile(h.ap())

    def dram(self, shape, dt=F32, name=None):
        return Tile(self.nc.dram_tensor(name or self.uid("dr"), list(shape), dt).ap())

    def dma(self, out, in_, eng="gpsimd"):
        o, i = (out.ap if isinstance(out, Tile) else out), (in_.ap if isinstance(in_, Tile) else in_)
        r = [in_] if isinstance(in_, Tile) else []
        w = [out] if isinstance(out, Tile) else []
        return self.em.op(eng, lambda E: E.dma_start(out=o, in_=i), reads=r, writes=w, dma=True)

    def mm(self, out, lhsT, rhs, start=True, stop=True):
        return self.em.op("tensor",
                          lambda E: E.matmul(out.ap, lhsT.ap, rhs.ap, start=start, stop=stop),
                          reads=[lhsT, rhs], writes=[out])

    def act(self, out, in_, func, scale=None, bias=None, accum=None, eng="scalar"):
        kw = {}
        if scale is not None:
            kw["scale"] = scale.ap if isinstance(scale, Tile) else np.float32(scale)
        if bias is not None:
            if isinstance(bias, Tile):
                kw["bias"] = bias.ap
            elif func in (AF.Copy, AF.Identity):
                kw["bias"] = float(bias)
            else:
                kw["bias"] = np.float32(bias)
        if accum is not None:
            kw["accum_out"] = accum.ap
        reads = [in_] + [t for t in (scale, bias) if isinstance(t, Tile)]
        writes = [out] + ([accum] if accum is not None else [])
        return self.em.op(eng, lambda E: E.activation(out.ap, in_.ap, func, **kw),
                          reads=reads, writes=writes)

    def tt(self, out, a, b, op, eng="vector"):
        return self.em.op(eng, lambda E: E.tensor_tensor(out=out.ap, in0=a.ap, in1=b.ap, op=op),
                          reads=[a, b], writes=[out])

    def ts(self, out, a, s1, op0, s2=None, op1=None, eng="vector"):
        def f(E):
            kw = {}
            if s2 is not None:
                return E.tensor_scalar(out=out.ap, in0=a.ap,
                                       scalar1=(s1.ap if isinstance(s1, Tile) else s1),
                                       scalar2=(s2.ap if isinstance(s2, Tile) else s2),
                                       op0=op0, op1=op1)
            return E.tensor_scalar(out=out.ap, in0=a.ap,
                                   scalar1=(s1.ap if isinstance(s1, Tile) else s1),
                                   scalar2=None, op0=op0)
        reads = [a] + [t for t in (s1, s2) if isinstance(t, Tile)]
        return self.em.op(eng, f, reads=reads, writes=[out])

    def stt(self, out, in0, scalar, in1, op0=OP.mult, op1=OP.add, eng="vector"):
        s = scalar.ap if isinstance(scalar, Tile) else scalar
        reads = [in0, in1] + ([scalar] if isinstance(scalar, Tile) else [])
        return self.em.op(eng,
                          lambda E: E.scalar_tensor_tensor(out=out.ap, in0=in0.ap, scalar=s,
                                                           in1=in1.ap, op0=op0, op1=op1),
                          reads=reads, writes=[out])

    def scan(self, out, a, b, initial, op0=OP.mult, op1=OP.add):
        init = initial.ap if isinstance(initial, Tile) else initial
        reads = [a, b] + ([initial] if isinstance(initial, Tile) else [])
        return self.em.op("vector",
                          lambda E: E.tensor_tensor_scan(out=out.ap, data0=a.ap, data1=b.ap,
                                                         initial=init, op0=op0, op1=op1),
                          reads=reads, writes=[out])

    def red(self, out, in_, op, absval=False, negate=False, eng="vector"):
        return self.em.op(eng,
                          lambda E: E.tensor_reduce(out=out.ap, in_=in_.ap, axis=AX.X, op=op,
                                                    apply_absolute_value=absval, negate=negate),
                          reads=[in_], writes=[out])

    def copy(self, out, in_, eng="vector"):
        if eng == "scalar":
            return self.em.op(eng, lambda E: E.copy(out=out.ap, in_=in_.ap),
                              reads=[in_], writes=[out])
        return self.em.op(eng, lambda E: E.tensor_copy(out=out.ap, in_=in_.ap),
                          reads=[in_], writes=[out])

    def cast(self, out, in_):
        return self.em.op("scalar", lambda E: E.copy(out=out.ap, in_=in_.ap),
                          reads=[in_], writes=[out])

    def memset(self, out, v, eng="vector"):
        return self.em.op(eng, lambda E: E.memset(out.ap, v), writes=[out])

    def recip(self, out, in_):
        return self.em.op("vector", lambda E: E.reciprocal(out=out.ap, in_=in_.ap),
                          reads=[in_], writes=[out])

    def transpose(self, out_ps, in_, ident):
        return self.em.op("tensor", lambda E: E.transpose(out_ps.ap, in_.ap, ident.ap),
                          reads=[in_, ident], writes=[out_ps])


SERIAL_MODE = True


def build_program(body_fn, in_specs, out_specs, n_banks=8):
    nc = bass.Bass("TRN2", target_bir_lowering=False, debug=False)
    ins = {k: Tile(nc.dram_tensor(k, list(sh), dt, kind="ExternalInput").ap())
           for k, (sh, dt) in in_specs.items()}
    outs = {k: Tile(nc.dram_tensor(k, list(sh), dt, kind="ExternalOutput").ap())
            for k, (sh, dt) in out_specs.items()}
    with ExitStack() as st:
        ctx = Ctx(nc, st)
        ctx.pbanks = [ctx.ps() for _ in range(n_banks)]
        sems = {e: st.enter_context(nc.semaphore(f"s_{e}")) for e in ENGS}
        dsems = {e: st.enter_context(nc.semaphore(f"d_{e}")) for e in ENGS}
        warm = ctx.sb(1, 8, F32)
        ctx.memset(warm, 1.0)
        for f in (AF.Exp, AF.Ln, AF.Silu, AF.Sigmoid, AF.Square, AF.Sqrt, AF.Relu):
            ctx.act(warm, warm, f)
        wdr = ctx.dram((1, 8), F32)
        for _ in range(4):
            ctx.dma(wdr, warm)
            ctx.dma(warm, wdr)
        ctx.barrier()
        body_fn(ctx, ins, outs)
        ctx.em.wait_all("sync")
        ctx.em.finalize(serial=SERIAL_MODE)
        with nc.Block() as block:
            ctx.em.emit(block, sems, dsems)
    return nc



def load_cast_bf16(ctx, dram_t, rows, cols, out_tiles=None, scratch=None):
    nt = rows // 128
    tiles = out_tiles if out_tiles is not None else [ctx.sb(128, cols, BF16) for _ in range(nt)]
    for i in range(nt):
        ctx.dma(tiles[i], dram_t[i * 128:(i + 1) * 128, :], eng="gpsimd")
    return tiles


def layernorm(ctx, x_tiles, ones_col, eps=1e-5, out_dt=BF16, out_tiles=None):
    ps_s, ps_q, ps_b, ps_r = ctx.pbanks[0], ctx.pbanks[1], ctx.pbanks[2], ctx.pbanks[3]
    sq = [ctx.scr_f(0), ctx.scr_f(1)]
    if not hasattr(ctx, "_ln_rows"):
        ctx._ln_rows = [ctx.sb(1, T, F32), ctx.sb(1, T, F32),
                        ctx.sb(1, TC, F32), ctx.sb(1, TC, F32)]
    m_row, rstd_row, v_row, tmp_row = ctx._ln_rows
    t1 = ctx.scr_f(2)
    xn = out_tiles if out_tiles is not None else [ctx.sb(128, T, out_dt) for _ in range(PT)]
    for c in range(NTC):
        sl = slice(c * TC, (c + 1) * TC)
        for p in range(PT):
            ctx.mm(ps_s[0:1, :], ones_col, x_tiles[p][:, sl], start=(p == 0), stop=(p == PT - 1))
        for p in range(PT):
            ctx.act(sq[p % 2], x_tiles[p][:, sl], AF.Square)
            ctx.mm(ps_q[0:1, :], ones_col, sq[p % 2], start=(p == 0), stop=(p == PT - 1))
        ctx.copy(m_row[:, sl], ps_s[0:1, :])
        ctx.act(tmp_row, m_row[:, sl], AF.Square)
        ctx.tt(v_row, ps_q[0:1, :], tmp_row, OP.subtract)
        ctx.ts(v_row, v_row, 0.0, OP.max)
        ctx.act(tmp_row, v_row, AF.Sqrt, bias=eps)
        ctx.recip(rstd_row[:, sl], tmp_row)
        ctx.mm(ps_b, ctx.ones_row_f32, m_row[:, sl])
        ctx.mm(ps_r, ctx.ones_row_f32, rstd_row[:, sl])
        for p in range(PT):
            ctx.tt(t1, x_tiles[p][:, sl], ps_b[:, :], OP.subtract)
            ctx.tt(xn[p][:, sl], t1, ps_r[:, :], OP.mult)
    return xn, m_row, rstd_row


def setup_consts(ctx, ins):
    ctx.ones_row_f32 = ctx.sb(1, 128, F32)
    ones_col = ctx.sb(128, 1, F32)
    ctx.dma(ctx.ones_row_f32, ins["c_ones128"])
    ctx.dma(ones_col, ins["c_onesD"])
    return ones_col


CONST_INPUTS = {
    "c_ones128": np.ones((1, 128), np.float32),
    "c_onesD": np.full((128, 1), 1.0 / D, np.float32),
}


def project(ctx, wtiles, rhs_fn, out_fn, nk=PT, no=PT, ntc=NTC):
    for c in range(ntc):
        for k in range(nk):
            rhs = rhs_fn(k, c)
            for o in range(no):
                ctx.mm(ctx.pbanks[o], wtiles[k][:, o * 128:(o + 1) * 128], rhs,
                       start=(k == 0), stop=(k == nk - 1))
        for o in range(no):
            out_fn(o, c, ctx.pbanks[o])



def body_l1ffn(ctx, ins, outs):
    ones_col = setup_consts(ctx, ins)
    x = [ctx.sb(128, T, F32) for _ in range(PT)]
    for p in range(PT):
        ctx.dma(x[p], ins["x"][p * 128:(p + 1) * 128, :])

    mu_r = ctx.sb(128, PT, F32)
    mu_k = ctx.sb(128, PT, F32)
    ctx.dma(mu_r, ins["mu_r"])
    ctx.dma(mu_k, ins["mu_k"])

    xn_pad = [ctx.sb(128, T + 1, BF16) for _ in range(PT)]
    for p in range(PT):
        ctx.memset(xn_pad[p][:, 0:1], 0.0)
    xn = layernorm(ctx, x, ones_col, out_tiles=[t[:, 1:T + 1] for t in xn_pad])[0]
    l1ffn_core(ctx, x, xn_pad, ins["W_rT"], ins["W_kT"], ins["W_vT"], mu_r, mu_k, scale=0.5)

    for p in range(PT):
        ctx.dma(outs["y"][p * 128:(p + 1) * 128, :], x[p])


def l1ffn_core(ctx, x, xn, WrT_d, WkT_d, WvT_d, mu_r, mu_k, scale=0.5,
               bufs=None):
    if bufs is None:
        bufs = {}
    kact_c = bufs.get("kact_c") or [ctx.sb(128, TC, BF16) for _ in range(PT)]
    r_c = bufs.get("r_c") or [ctx.sb(128, TC, BF16) for _ in range(PT)]
    wk = bufs.get("wk") or [ctx.sb(128, D, BF16) for _ in range(PT)]
    wr = bufs.get("wr") or [ctx.sb(128, D, BF16) for _ in range(PT)]
    wv = bufs.get("wv") or [ctx.sb(128, D, BF16) for _ in range(PT)]
    load_cast_bf16(ctx, WkT_d, D, D, out_tiles=wk)
    load_cast_bf16(ctx, WrT_d, D, D, out_tiles=wr)
    load_cast_bf16(ctx, WvT_d, D, D, out_tiles=wv)

    def blended(mu):
        def f(k, c):
            cur = xn[k][:, 1 + c * TC: 1 + (c + 1) * TC]
            prev = xn[k][:, c * TC: (c + 1) * TC]
            d = ctx.scr_b(k % 2)
            rhs = ctx.scr_b(2 + k % 2)
            ctx.tt(d, cur, prev, OP.subtract)
            ctx.stt(rhs, d, mu[:, k:k + 1], prev, OP.mult, OP.add)
            return rhs
        return f

    bk, br = blended(mu_k), blended(mu_r)
    for c in range(NTC):
        sl = slice(c * TC, (c + 1) * TC)
        for k in range(PT):
            rhs = bk(k, c)
            for o in range(PT):
                ctx.mm(ctx.pbanks[o], wk[k][:, o * 128:(o + 1) * 128], rhs,
                       start=(k == 0), stop=(k == PT - 1))
        for o in range(PT):
            t = ctx.scr_b(4 + o % 2)
            ctx.act(t, ctx.pbanks[o], AF.Relu)
            ctx.act(kact_c[o], t, AF.Square)
        for k in range(PT):
            rhs = br(k, c)
            for o in range(PT):
                ctx.mm(ctx.pbanks[o], wr[k][:, o * 128:(o + 1) * 128], rhs,
                       start=(k == 0), stop=(k == PT - 1))
        for o in range(PT):
            ctx.act(r_c[o], ctx.pbanks[o], AF.Sigmoid)
        for k in range(PT):
            for o in range(PT):
                ctx.mm(ctx.pbanks[o], wv[k][:, o * 128:(o + 1) * 128], kact_c[k],
                       start=(k == 0), stop=(k == PT - 1))
        for o in range(PT):
            vt = ctx.scr_b(4 + o % 2)
            ctx.tt(vt, r_c[o], ctx.pbanks[o], OP.mult)
            ctx.stt(x[o][:, sl], vt, scale, x[o][:, sl], OP.mult, OP.add)



I_FFN = 2816
IT = I_FFN // 128


def absmax_scalar(ctx, tiles, ident_f, ps_bank, out_row):
    cols = ctx.sb(128, len(tiles), F32)
    col = ctx.sb(128, 1, F32)
    for i, t in enumerate(tiles):
        ctx.red(cols[:, i:i + 1], t, OP.max, absval=True)
    ctx.red(col, cols, OP.max)
    ctx.transpose(ps_bank[0:1, 0:128], col, ident_f)
    ctx.red(out_row, ps_bank[0:1, 0:128], OP.max)


def quant_chunks(ctx, dst, src, inv_col, ncols, cw=TC):
    c0 = 0
    while c0 < ncols:
        w = min(cw, ncols - c0)
        sl = slice(c0, c0 + w)
        f = ctx.scr_f(0)
        i = ctx.scr_i(0)
        ctx.ts(f[:, 0:w], src[:, sl], inv_col, OP.mult)
        ctx.copy(i[:, 0:w], f[:, 0:w])
        ctx.ts(dst[:, sl], i[:, 0:w], -128.0, OP.max, 127.0, OP.min)
        c0 += w


def scalar_bcast(ctx, ps_bank, row11, out_col):
    ctx.mm(ps_bank[:, 0:1], ctx.ones_row_f32, row11)
    ctx.copy(out_col, ps_bank[:, 0:1])


def qt_weight(ctx, w_d, wqT_d, sc_d, R, C, ident_b):
    RT, CT = R // 128, C // 128
    wscr = [ctx.sb(128, C, F32) for _ in range(3)]
    wq = [ctx.sb(128, C, BF16) for _ in range(RT)]
    scol = ctx.sb(128, RT, F32)
    mx = ctx.sb(128, 1, F32)
    inv = ctx.sb(128, 1, F32)
    out_sb = ctx.sb(128, R, BF16)
    psT = ctx.ps(128, 128, BF16)
    for rt in range(RT):
        s = wscr[rt % 3]
        ctx.dma(s, w_d[rt * 128:(rt + 1) * 128, :])
        ctx.red(mx, s, OP.max, absval=True)
        ctx.ts(scol[:, rt:rt + 1], mx, 1.0 / 127.0, OP.mult)
        ctx.recip(inv, scol[:, rt:rt + 1])
        quant_chunks(ctx, wq[rt], s, inv, C)
    for j in range(CT):
        for rt in range(RT):
            ctx.transpose(psT, wq[rt][:, j * 128:(j + 1) * 128], ident_b)
            ctx.copy(out_sb[:, rt * 128:(rt + 1) * 128], psT)
        ctx.dma(wqT_d[j * 128:(j + 1) * 128, :], out_sb)
    ctx.dma(sc_d, scol)


def body_ffn0_prep(ctx, ins, outs):
    setup_consts(ctx, ins)
    ident_b = ctx.sb(128, 128, BF16)
    ctx.dma(ident_b, ins["c_eyeb"])
    qt_weight(ctx, ins["w_gate"], outs["wgT"], outs["sg"], I_FFN, D, ident_b)
    qt_weight(ctx, ins["w_up"], outs["wuT"], outs["su"], I_FFN, D, ident_b)
    qt_weight(ctx, ins["w_down"], outs["wdT"], outs["sdn"], D, I_FFN, ident_b)


def body_ffn0_a(ctx, ins, outs):
    ones_col = setup_consts(ctx, ins)
    ident_f = ctx.sb(128, 128, F32)
    ctx.dma(ident_f, ins["c_eyef"])
    x = [ctx.sb(128, T, F32) for _ in range(PT)]
    for p in range(PT):
        ctx.dma(x[p], ins["x"][p * 128:(p + 1) * 128, :])
    xn = layernorm(ctx, x, ones_col, out_dt=F32, out_tiles=x)[0]
    lmax = ctx.sb(1, 1, F32)
    absmax_scalar(ctx, xn, ident_f, ctx.pbanks[4], lmax)
    for p in range(PT):
        ctx.dma(outs["xn"][p * 128:(p + 1) * 128, :], xn[p])
    ctx.dma(outs["lmax"], lmax)


def body_ffn0_b(ctx, ins, outs):
    setup_consts(ctx, ins)
    ident_f = ctx.sb(128, 128, F32)
    ctx.dma(ident_f, ins["c_eyef"])
    xnst = [ctx.sb(128, T, F32) for _ in range(4)]
    xq = [ctx.sb(128, T, BF16) for _ in range(PT)]
    wg = [ctx.sb(128, I_FFN, BF16) for _ in range(PT)]
    wu = [ctx.sb(128, I_FFN, BF16) for _ in range(PT)]
    for k in range(PT):
        ctx.dma(wg[k], ins["wgT"][k * 128:(k + 1) * 128, :])
        ctx.dma(wu[k], ins["wuT"][k * 128:(k + 1) * 128, :])
    sg = ctx.sb(128, IT, F32)
    su = ctx.sb(128, IT, F32)
    ctx.dma(sg, ins["sg"])
    ctx.dma(su, ins["su"])

    g8 = ctx.sb(1, NCORE, F32)
    clip_sb = ctx.sb(1, 1, F32)
    gmax = ctx.sb(1, 1, F32)
    sx = ctx.sb(1, 1, F32)
    inv_sx = ctx.sb(1, 1, F32)
    ctx.dma(g8, ins["gmax8"])
    ctx.dma(clip_sb, ins["clip_x"])
    ctx.red(gmax, g8, OP.max)
    ctx.act(sx, clip_sb, AF.Sigmoid)
    ctx.tt(sx, sx, gmax, OP.mult)
    ctx.ts(sx, sx, 1.0 / 127.0, OP.mult)
    ctx.recip(inv_sx, sx)
    invb = ctx.sb(128, 1, F32)
    sxb = ctx.sb(128, 1, F32)
    scalar_bcast(ctx, ctx.pbanks[4], inv_sx, invb)
    scalar_bcast(ctx, ctx.pbanks[4], sx, sxb)
    sxg = ctx.sb(128, IT, F32)
    sxu = ctx.sb(128, IT, F32)
    ctx.ts(sxg, sg, sxb, OP.mult)
    ctx.ts(sxu, su, sxb, OP.mult)

    for p in range(PT):
        st = xnst[p % 4]
        ctx.dma(st, ins["xn"][p * 128:(p + 1) * 128, :])
        quant_chunks(ctx, xq[p], st, invb, T)

    hmax_cols = ctx.sb(128, IT * NTC, F32)
    for o in range(IT):
        for c in range(NTC):
            sl = slice(c * TC, (c + 1) * TC)
            psg = ctx.pbanks[2 * (c % 2)]
            psu = ctx.pbanks[2 * (c % 2) + 1]
            e1 = ctx.scr_f(c % 2)
            e2 = ctx.scr_f(2 + c % 2)
            for k in range(4):
                ctx.mm(psg, wg[k][:, o * 128:(o + 1) * 128], xq[k][:, sl],
                       start=(k == 0), stop=(k == 3))
            ctx.copy(e1, psg, eng="scalar")
            for k in range(4, PT):
                ctx.mm(psg, wg[k][:, o * 128:(o + 1) * 128], xq[k][:, sl],
                       start=(k == 4), stop=(k == PT - 1))
            ctx.tt(e1, e1, psg, OP.add)
            for k in range(4):
                ctx.mm(psu, wu[k][:, o * 128:(o + 1) * 128], xq[k][:, sl],
                       start=(k == 0), stop=(k == 3))
            ctx.copy(e2, psu, eng="scalar")
            for k in range(4, PT):
                ctx.mm(psu, wu[k][:, o * 128:(o + 1) * 128], xq[k][:, sl],
                       start=(k == 4), stop=(k == PT - 1))
            ctx.tt(e2, e2, psu, OP.add)
            gs = ctx.scr_f(4 + c % 2)
            ht = ctx.scr_f(6 + (o * NTC + c) % 4)
            ctx.act(gs, e1, AF.Silu, scale=sxg[:, o:o + 1])
            ctx.ts(e2, e2, sxu[:, o:o + 1], OP.mult)
            ctx.tt(ht, gs, e2, OP.mult)
            ctx.red(hmax_cols[:, o * NTC + c:o * NTC + c + 1], ht, OP.max, absval=True)
            ctx.dma(outs["h"][o * 128:(o + 1) * 128, sl], ht)
    col = ctx.sb(128, 1, F32)
    lmax = ctx.sb(1, 1, F32)
    ctx.red(col, hmax_cols, OP.max)
    ctx.transpose(ctx.pbanks[4][0:1, 0:128], col, ident_f)
    ctx.red(lmax, ctx.pbanks[4][0:1, 0:128], OP.max)
    ctx.dma(outs["hmax"], lmax)


def body_ffn0_c(ctx, ins, outs):
    setup_consts(ctx, ins)
    sdn = ctx.sb(128, PT, F32)
    ctx.dma(sdn, ins["sdn"])

    g8 = ctx.sb(1, NCORE, F32)
    clip_sb = ctx.sb(1, 1, F32)
    gmax = ctx.sb(1, 1, F32)
    sh = ctx.sb(1, 1, F32)
    inv_sh = ctx.sb(1, 1, F32)
    ctx.dma(g8, ins["hmax8"])
    ctx.dma(clip_sb, ins["clip_down"])
    ctx.red(gmax, g8, OP.max)
    ctx.act(sh, clip_sb, AF.Sigmoid)
    ctx.tt(sh, sh, gmax, OP.mult)
    ctx.ts(sh, sh, 1.0 / 127.0, OP.mult)
    ctx.recip(inv_sh, sh)
    invb = ctx.sb(128, 1, F32)
    shb = ctx.sb(128, 1, F32)
    scalar_bcast(ctx, ctx.pbanks[0], inv_sh, invb)
    scalar_bcast(ctx, ctx.pbanks[0], sh, shb)
    eff = ctx.sb(128, PT, F32)
    ctx.ts(eff, sdn, shb, OP.mult, 0.5, OP.mult)

    wd = [ctx.sb(128, D, BF16) for _ in range(IT)]
    hq = [ctx.sb(128, T, BF16) for _ in range(IT)]
    hst = [ctx.sb(128, T, F32) for _ in range(3)]
    for k in range(IT):
        ctx.dma(wd[k], ins["wdT"][k * 128:(k + 1) * 128, :])
        ctx.dma(hst[k % 3], ins["h"][k * 128:(k + 1) * 128, :])
        quant_chunks(ctx, hq[k], hst[k % 3], invb, T)

    xst = [ctx.sb(128, TC, F32) for _ in range(4)]
    yst = [ctx.sb(128, TC, F32) for _ in range(4)]
    for c in range(NTC):
        sl = slice(c * TC, (c + 1) * TC)
        for k in range(IT):
            for o in range(PT):
                ctx.mm(ctx.pbanks[o], wd[k][:, o * 128:(o + 1) * 128], hq[k][:, sl],
                       start=(k == 0), stop=(k == IT - 1))
        for o in range(PT):
            t = ctx.scr_b(o % 2)
            xc = xst[o % 4]
            yo = yst[o % 4]
            ctx.act(t, ctx.pbanks[o], AF.Copy, scale=eff[:, o:o + 1], bias=0.0)
            ctx.dma(xc, ins["x"][o * 128:(o + 1) * 128, sl])
            ctx.tt(yo, xc, t, OP.add)
            ctx.dma(outs["y"][o * 128:(o + 1) * 128, sl], yo)



INNER = 2048
CTI = INNER // 128


def exp_silu(ctx, out, in_t, sA, sB):
    ctx.act(sA, in_t, AF.Exp, scale=-1.0)
    ctx.ts(sA, sA, 1.0, OP.add)
    ctx.recip(sB, sA)
    ctx.tt(out, in_t, sB, OP.mult)


def body_mamba(ctx, ins, outs):
    ones_col = setup_consts(ctx, ins)
    xd = ins["x"]

    xbuf = [ctx.sb(128, TC, F32) for _ in range(4)]

    def xprov(p, c, bi):
        b = xbuf[bi % 4]
        ctx.dma(b, xd[p * 128:(p + 1) * 128, c * TC:(c + 1) * TC])
        return b

    xn = layernorm_str(ctx, xprov, ones_col)

    cw = ctx.sb(128, CTI * 4, F32)
    cb = ctx.sb(128, CTI, F32)
    Dp = ctx.sb(128, CTI, F32)
    alog = ctx.sb(128, CTI * 16, F32)
    A_sb = ctx.sb(128, CTI * 16, F32)
    ctx.dma(cw, ins["conv_w"])
    ctx.dma(cb, ins["conv_b"])
    ctx.dma(Dp, ins["D"])
    ctx.dma(alog, ins["A_log"])
    ctx.act(A_sb, alog, AF.Exp)
    ctx.ts(A_sb, A_sb, -1.0, OP.mult)

    wsl = [[ctx.sb(128, 256, BF16) for _ in range(PT)] for _ in range(2)]
    y_full = [ctx.sb(128, T, BF16) for _ in range(CTI)]

    xssm = ctx.sb(128, T + 3, BF16)
    xconv = ctx.sb(128, T, BF16)
    delta = ctx.sb(128, T, BF16)
    Bx = ctx.sb(128, T, BF16)
    a_d = ctx.sb(128, T, BF16)
    h_t = ctx.sb(128, T, BF16)
    ysum = ctx.sb(128, T, BF16)
    res_t = ctx.sb(128, T, BF16)
    sres = ctx.sb(128, T, BF16)
    t_a = ctx.sb(128, T, BF16)
    t_b = ctx.sb(128, T, BF16)
    s_f1 = ctx.sb(128, T, F32)
    s_f2 = ctx.sb(128, T, F32)

    ctx.memset(xssm[:, 0:3], 0.0)
    WinT = ins["W_inT"]

    for i in range(CTI):
        ws = wsl[i % 2]
        for k in range(PT):
            ctx.dma(ws[k][:, 0:128], WinT[k * 128:(k + 1) * 128, i * 128:(i + 1) * 128], eng="gpsimd")
            ctx.dma(ws[k][:, 128:256], WinT[k * 128:(k + 1) * 128, INNER + i * 128:INNER + (i + 1) * 128], eng="gpsimd")
        for c in range(NTC):
            sl = slice(c * TC, (c + 1) * TC)
            psA = ctx.pbanks[2 * (c % 2)]
            psB = ctx.pbanks[2 * (c % 2) + 1]
            for k in range(PT):
                ctx.mm(psA, ws[k][:, 0:128], xn[k][:, sl], start=(k == 0), stop=(k == PT - 1))
            for k in range(PT):
                ctx.mm(psB, ws[k][:, 128:256], xn[k][:, sl], start=(k == 0), stop=(k == PT - 1))
            ctx.cast(xssm[:, 3 + c * TC:3 + (c + 1) * TC], psA)
            ctx.cast(res_t[:, sl], psB)
        ctx.ts(t_a, xssm[:, 0:T], cw[:, 4 * i:4 * i + 1], OP.mult)
        for j in range(1, 4):
            ctx.stt(t_a, xssm[:, j:j + T], cw[:, 4 * i + j:4 * i + j + 1], t_a, OP.mult, OP.add)
        ctx.ts(t_b, t_a, cb[:, i:i + 1], OP.add)
        exp_silu(ctx, xconv, t_b, s_f1, s_f2)
        ctx.act(t_a, xconv, AF.Exp)
        ctx.act(delta, t_a, AF.Ln, bias=1.0)
        ctx.act(Bx, xconv, AF.Square)
        for n in range(16):
            ctx.act(a_d, delta, AF.Exp, scale=A_sb[:, i * 16 + n:i * 16 + n + 1])
            ctx.scan(h_t, a_d, Bx, 0.0)
            if n == 0:
                ctx.copy(ysum, h_t)
            else:
                ctx.tt(ysum, ysum, h_t, OP.add)
        exp_silu(ctx, sres, res_t, s_f1, s_f2)
        ctx.tt(h_t, xconv, ysum, OP.mult)
        ctx.tt(h_t, h_t, sres, OP.mult)
        ctx.stt(y_full[i], xconv, Dp[:, i:i + 1], h_t, OP.mult, OP.add)

    WoT = ins["W_outT"]
    wo = [ctx.sb(128, D, BF16) for _ in range(4)]
    for c in range(NTC):
        sl = slice(c * TC, (c + 1) * TC)
        for k in range(CTI):
            ctx.dma(wo[k % 4], WoT[k * 128:(k + 1) * 128, :], eng="gpsimd")
            for o in range(PT):
                ctx.mm(ctx.pbanks[o], wo[k % 4][:, o * 128:(o + 1) * 128], y_full[k][:, sl],
                       start=(k == 0), stop=(k == CTI - 1))
        for o in range(PT):
            xc = xbuf[o % 4]
            ctx.dma(xc, xd[o * 128:(o + 1) * 128, sl])
            yo = ctx.scr_f(o % 4)
            ctx.tt(yo, xc, ctx.pbanks[o], OP.add)
            ctx.dma(outs["y"][o * 128:(o + 1) * 128, sl], yo)


def layernorm_str(ctx, xprov, ones_col, eps=1e-5, out_dt=BF16, out_tiles=None):
    ps_s, ps_q, ps_b, ps_r = ctx.pbanks[0], ctx.pbanks[1], ctx.pbanks[2], ctx.pbanks[3]
    sq = ctx.scr_f(0)
    if not hasattr(ctx, "_ln_rows_c"):
        ctx._ln_rows_c = [ctx.sb(1, TC, F32), ctx.sb(1, TC, F32),
                          ctx.sb(1, TC, F32), ctx.sb(1, TC, F32)]
    m_row, rstd_row, v_row, tmp_row = ctx._ln_rows_c
    xn = out_tiles if out_tiles is not None else [ctx.sb(128, T, out_dt) for _ in range(PT)]
    for c in range(NTC):
        sl = slice(c * TC, (c + 1) * TC)
        for p in range(PT):
            xt = xprov(p, c, p)
            ctx.mm(ps_s[0:1, :], ones_col, xt, start=(p == 0), stop=(p == PT - 1))
            ctx.act(sq, xt, AF.Square)
            ctx.mm(ps_q[0:1, :], ones_col, sq, start=(p == 0), stop=(p == PT - 1))
        ctx.copy(m_row, ps_s[0:1, :])
        ctx.act(tmp_row, m_row, AF.Square)
        ctx.tt(v_row, ps_q[0:1, :], tmp_row, OP.subtract)
        ctx.ts(v_row, v_row, 0.0, OP.max)
        ctx.act(tmp_row, v_row, AF.Sqrt, bias=eps)
        ctx.recip(rstd_row, tmp_row)
        ctx.mm(ps_b, ctx.ones_row_f32, m_row)
        ctx.mm(ps_r, ctx.ones_row_f32, rstd_row)
        for p in range(PT):
            xt = xprov(p, c, p)
            t1 = ctx.scr_f(0)
            ctx.tt(t1, xt, ps_b[:, :], OP.subtract)
            ctx.tt(xn[p][:, sl], t1, ps_r[:, :], OP.mult)
    return xn



NH = 16
HD = 64
NTCH = T // 128


def body_l1attn_a(ctx, ins, outs):
    ones_col = setup_consts(ctx, ins)
    xd = ins["x"]
    xbuf = [ctx.sb(128, TC, F32) for _ in range(4)]

    def xprov(p, c, bi):
        b = xbuf[bi % 4]
        ctx.dma(b, xd[p * 128:(p + 1) * 128, c * TC:(c + 1) * TC])
        return b

    xn = layernorm_str(ctx, xprov, ones_col)

    qw = [ctx.sb(128, D, BF16) for _ in range(PT)]
    load_cast_bf16(ctx, ins["qw"], D, D, out_tiles=qw)

    q = [ctx.sb(128, T, BF16) for _ in range(PT)]
    for c in range(NTC):
        sl = slice(c * TC, (c + 1) * TC)
        for j in range(PT):
            ps = ctx.pbanks[4 + j % 2]
            for k in range(PT):
                ctx.mm(ps, qw[k][:, j * 128:(j + 1) * 128], xn[k][:, sl],
                       start=(k == 0), stop=(k == PT - 1))
            ctx.cast(q[j][:, sl], ps)
    for p in range(PT):
        ctx.dma(outs["q"][p * 128:(p + 1) * 128, :], q[p])

    hpat = ctx.sb(128, NH * PT, BF16)
    ctx.dma(hpat, ins["hpat"])
    psK = ctx.pbanks[6]
    sq = [ctx.sb(128, T, BF16) for _ in range(PT)]
    for j in range(PT):
        ctx.act(sq[j], q[j], AF.Square)
    ksq = ctx.sb(128, NTCH * NH, F32)
    for tc in range(NTCH):
        for j in range(PT):
            ctx.mm(psK[:, 0:NH], sq[j][:, tc * 128:(tc + 1) * 128],
                   hpat[:, j * NH:(j + 1) * NH], start=(j == 0), stop=(j == PT - 1))
        ctx.copy(ksq[:, tc * NH:(tc + 1) * NH], psK[:, 0:NH])
    ctx.dma(outs["ksq"], ksq)


def body_l1attn_b(ctx, ins, outs):
    setup_consts(ctx, ins)
    hpat = ctx.sb(128, NH * PT, BF16)
    ctx.dma(hpat, ins["hpat"])
    q_me = [ctx.sb(128, T, BF16) for _ in range(PT)]
    for p in range(PT):
        ctx.dma(q_me[p], ins["q_me"][p * 128:(p + 1) * 128, :])
    ksq_sb = [ctx.sb(128, NTCH * NH, F32) for _ in range(NCORE)]
    for s in range(NCORE):
        ctx.dma(ksq_sb[s], ins["ksq_all"][s * 128:(s + 1) * 128, :])

    qs = [ctx.sb(128, T, BF16) for _ in range(PT)]
    prod = [ctx.sb(128, T, BF16) for _ in range(PT)]
    Racc = [ctx.sb(128, NTCH * NH, F32) for _ in range(NCORE)]
    psR = ctx.pbanks[0]
    t256 = ctx.sb(128, NTCH * NH, F32)
    for s in range(NCORE):
        for j in range(PT):
            ctx.dma(qs[j], ins["q_all"][s * D + j * 128:s * D + (j + 1) * 128, :])
            ctx.tt(prod[j], q_me[j], qs[j], OP.mult)
        for tc in range(NTCH):
            for j in range(PT):
                ctx.mm(psR[:, 0:NH], prod[j][:, tc * 128:(tc + 1) * 128],
                       hpat[:, j * NH:(j + 1) * NH], start=(j == 0), stop=(j == PT - 1))
            cs = slice(tc * NH, (tc + 1) * NH)
            ctx.ts(t256[:, cs], psR[:, 0:NH], 0.25, OP.mult)
            ctx.stt(Racc[s][:, cs], ksq_sb[s][:, cs], -0.125, t256[:, cs], OP.mult, OP.add)

    ctx.barrier()
    mxp = [ctx.sb(128, NTCH * NH, F32) for _ in range(2)]
    smp = [ctx.sb(128, NTCH * NH, F32) for _ in range(2)]
    rn = ctx.sb(128, NTCH * NH, F32)
    E = [ctx.sb(128, NTCH * NH, BF16) for _ in range(NCORE)]
    P = [ctx.sb(128, NTCH * NH, BF16) for _ in range(NCORE)]
    ctx.tt(mxp[0], Racc[0], Racc[1], OP.max)
    for s in range(2, NCORE):
        ctx.tt(mxp[(s - 1) % 2], mxp[s % 2], Racc[s], OP.max)
    mx = mxp[(NCORE - 2) % 2]
    for s in range(NCORE):
        ctx.tt(t256, Racc[s], mx, OP.subtract)
        ctx.act(E[s], t256, AF.Exp)
    ctx.tt(smp[0], E[0], E[1], OP.add)
    for s in range(2, NCORE):
        ctx.tt(smp[(s - 1) % 2], smp[s % 2], E[s], OP.add)
    ctx.recip(rn, smp[(NCORE - 2) % 2])
    for s in range(NCORE):
        ctx.tt(P[s], E[s], rn, OP.mult)

    ctx.barrier()
    qw = [prod[k][:, 0:D] for k in range(PT)]
    vw = [prod[k][:, D:2 * D] for k in range(PT)]
    for k in range(PT):
        ctx.dma(qw[k], ins["qw"][k * 128:(k + 1) * 128, :], eng="gpsimd")
        ctx.dma(vw[k], ins["vw"][k * 128:(k + 1) * 128, :], eng="gpsimd")
    psM = ctx.pbanks[1]
    M_sb = ctx.sb(64, NH * HD, F32)
    for h in range(NH):
        for k in range(PT):
            ctx.mm(psM[0:HD, 0:HD], qw[k][:, h * HD:(h + 1) * HD],
                   vw[k][:, h * HD:(h + 1) * HD], start=(k == 0), stop=(k == PT - 1))
        ctx.ts(M_sb[0:HD, h * HD:(h + 1) * HD], psM[0:HD, 0:HD], 0.125, OP.mult)
    BD = [ctx.sb(128, 128, BF16) for _ in range(PT)]
    for j in range(PT):
        ctx.memset(BD[j], 0.0)
        h0, h1 = 2 * j, 2 * j + 1
        ctx.dma(BD[j][0:HD, 0:HD], M_sb[0:HD, h0 * HD:(h0 + 1) * HD], eng="gpsimd")
        ctx.dma(BD[j][HD:128, HD:128], M_sb[0:HD, h1 * HD:(h1 + 1) * HD], eng="gpsimd")

    ctx.barrier()
    acc = [ctx.sb(128, D, BF16) for _ in range(NTCH)]
    u_sb = ctx.sb(128, D, BF16)
    tmp = ctx.sb(128, D, BF16)
    for s in range(NCORE):
        for j in range(PT):
            ctx.dma(qs[j], ins["q_all"][s * D + j * 128:s * D + (j + 1) * 128, :])
        for tc in range(NTCH):
            for j in range(PT):
                bank = ctx.pbanks[3 + (j // 4) % 2]
                ctx.mm(bank[:, (j % 4) * 128:(j % 4 + 1) * 128],
                       qs[j][:, tc * 128:(tc + 1) * 128], BD[j],
                       start=True, stop=True)
            ctx.cast(u_sb[:, 0:TC], ctx.pbanks[3])
            ctx.cast(u_sb[:, TC:D], ctx.pbanks[4])
            pb = Tile(P[s].ap[:, tc * NH:(tc + 1) * NH]
                      .rearrange("q (h o) -> q h o", o=1).broadcast_to([128, NH, HD]),
                      P[s].root)
            ctx.tt(Tile(tmp.ap.rearrange("q (h o) -> q h o", o=HD), tmp.root),
                   Tile(u_sb.ap.rearrange("q (h o) -> q h o", o=HD), u_sb.root),
                   pb, OP.mult)
            if s == 0:
                ctx.copy(acc[tc], tmp)
            else:
                ctx.tt(acc[tc], acc[tc], tmp, OP.add)

    ctx.barrier()
    ident_b = ctx.sb(128, 128, BF16)
    ctx.dma(ident_b, ins["c_eyeb"])
    ow = [q_me[k][:, 0:D] for k in range(PT)]
    load_cast_bf16(ctx, ins["out_wT"], D, D, out_tiles=ow)
    ob = ctx.sb(128, PT, F32)
    ctx.dma(ob, ins["out_b"])
    pxv = [qs[j][:, 0:TC] for j in range(PT)]
    psT = ctx.ps(128, 128, BF16)
    xcp = [ctx.sb(128, TC, F32) for _ in range(4)]
    yop = [ctx.sb(128, TC, F32) for _ in range(4)]
    for c in range(NTC):
        sl = slice(c * TC, (c + 1) * TC)
        for tci in range(4):
            tc = c * 4 + tci
            for j in range(PT):
                ctx.transpose(psT, acc[tc][:, j * 128:(j + 1) * 128], ident_b)
                ctx.copy(pxv[j][:, tci * 128:(tci + 1) * 128], psT)
        for o in range(PT):
            ps = ctx.pbanks[5 + o % 2]
            for k in range(PT):
                ctx.mm(ps, ow[k][:, o * 128:(o + 1) * 128], pxv[k],
                       start=(k == 0), stop=(k == PT - 1))
            t = ctx.scr_b(o % 2)
            xc = xcp[o % 4]
            yo = yop[o % 4]
            ctx.ts(t, ps, ob[:, o:o + 1], OP.add)
            ctx.dma(xc, ins["x"][o * 128:(o + 1) * 128, sl])
            ctx.tt(yo, xc, t, OP.add)
            ctx.dma(outs["y"][o * 128:(o + 1) * 128, sl], yo)


def run_l1attn(x_fm_cores, q_weight, v_weight, out_w, out_b):
    qw2d = np.ascontiguousarray(q_weight.reshape(D, D))
    vw2d = np.ascontiguousarray(v_weight.reshape(D, D))
    hpat = np.zeros((128, NH * PT), np.float32)
    for j in range(PT):
        hpat[0:HD, j * NH + 2 * j] = 1.0
        hpat[HD:128, j * NH + 2 * j + 1] = 1.0
    hpat = hpat.astype(BF)

    spec_in = {"x": ((D, T), F32), "qw": ((D, D), F32), "hpat": ((128, NH * PT), BF16),
               **_spec_consts()}
    spec_out = {"q": ((D, T), BF16), "ksq": ((128, NTCH * NH), F32)}
    nc = build_program(body_l1attn_a, spec_in, spec_out)
    shared = {"qw": qw2d, "hpat": hpat, **CONST_INPUTS}
    res = run_bass_kernel_spmd(nc, [{"x": xc, **shared} for xc in x_fm_cores],
                               core_ids=list(range(NCORE)))
    q_cores = [r["q"] for r in res.results]
    ksq_all = np.concatenate([r["ksq"] for r in res.results], axis=0)
    q_all = np.concatenate(q_cores, axis=0)

    spec_in = {"x": ((D, T), F32), "q_me": ((D, T), BF16),
               "q_all": ((NCORE * D, T), BF16), "ksq_all": ((NCORE * 128, NTCH * NH), F32),
               "qw": ((D, D), F32), "vw": ((D, D), F32), "out_wT": ((D, D), F32),
               "out_b": ((128, PT), F32), "hpat": ((128, NH * PT), BF16),
               "c_eyeb": ((128, 128), BF16), **_spec_consts()}
    spec_out = {"y": ((D, T), F32)}
    nc = build_program(body_l1attn_b, spec_in, spec_out, n_banks=7)
    shared = {"q_all": q_all, "ksq_all": ksq_all, "qw": qw2d, "vw": vw2d,
              "out_wT": np.ascontiguousarray(out_w.T), "out_b": _ppn(out_b, PT),
              "hpat": hpat, "c_eyeb": CONST_INPUTS_EYE["c_eyeb"], **CONST_INPUTS}
    res = run_bass_kernel_spmd(
        nc, [{"x": xc, "q_me": qc, **shared} for xc, qc in zip(x_fm_cores, q_cores)],
        core_ids=list(range(NCORE)))
    return [r["y"] for r in res.results]



def _fm(a):
    return np.ascontiguousarray(a.T)


def _pp(v):
    return np.ascontiguousarray(v.reshape(PT, 128).T)


def run_l1ffn(x_fm_cores, W_r, W_k, W_v, mu_r, mu_k):
    spec_in = {
        "x": ((D, T), F32), "W_rT": ((D, D), F32), "W_kT": ((D, D), F32),
        "W_vT": ((D, D), F32), "mu_r": ((128, PT), F32), "mu_k": ((128, PT), F32),
        "c_ones128": ((1, 128), F32), "c_onesD": ((128, 1), F32),
    }
    spec_out = {"y": ((D, T), F32)}
    nc = build_program(body_l1ffn, spec_in, spec_out)
    shared = {
        "W_rT": np.ascontiguousarray(W_r.T), "W_kT": np.ascontiguousarray(W_k.T),
        "W_vT": np.ascontiguousarray(W_v.T), "mu_r": _pp(mu_r), "mu_k": _pp(mu_k),
        **CONST_INPUTS,
    }
    in_maps = [{"x": xc, **shared} for xc in x_fm_cores]
    res = run_bass_kernel_spmd(nc, in_maps, core_ids=list(range(NCORE)))
    return [r["y"] for r in res.results]


def run_mamba(x_fm_cores, W_in, conv_w, conv_b, A_log, Dv, W_out):
    spec_in = {
        "x": ((D, T), F32), "W_inT": ((D, 2 * INNER), F32),
        "W_outT": ((INNER, D), F32), "conv_w": ((128, CTI * 4), F32),
        "conv_b": ((128, CTI), F32), "D": ((128, CTI), F32),
        "A_log": ((128, CTI * 16), F32), **_spec_consts(),
    }
    spec_out = {"y": ((D, T), F32)}
    nc = build_program(body_mamba, spec_in, spec_out)
    shared = {
        "W_inT": np.ascontiguousarray(W_in.T),
        "W_outT": np.ascontiguousarray(W_out.T),
        "conv_w": np.ascontiguousarray(conv_w.reshape(CTI, 128, 4).transpose(1, 0, 2).reshape(128, CTI * 4)),
        "conv_b": _ppn(conv_b, CTI), "D": _ppn(Dv, CTI),
        "A_log": np.ascontiguousarray(A_log.reshape(CTI, 128, 16).transpose(1, 0, 2).reshape(128, CTI * 16)),
        **CONST_INPUTS,
    }
    in_maps = [{"x": xc, **shared} for xc in x_fm_cores]
    res = run_bass_kernel_spmd(nc, in_maps, core_ids=list(range(NCORE)))
    return [r["y"] for r in res.results]


def _ppn(v, n):
    return np.ascontiguousarray(v.reshape(n, 128).T)


CONST_INPUTS_EYE = {
    "c_eyef": np.eye(128, dtype=np.float32),
    "c_eyeb": np.eye(128, dtype=BF),
}


def _spec_consts():
    return {"c_ones128": ((1, 128), F32), "c_onesD": ((128, 1), F32)}


def run_ffn0_prep(w_gate, w_up, w_down):
    spec_in = {"w_gate": ((I_FFN, D), F32), "w_up": ((I_FFN, D), F32),
               "w_down": ((D, I_FFN), F32), "c_eyeb": ((128, 128), BF16),
               **_spec_consts()}
    spec_out = {"wgT": ((D, I_FFN), BF16), "wuT": ((D, I_FFN), BF16),
                "wdT": ((I_FFN, D), BF16),
                "sg": ((128, IT), F32), "su": ((128, IT), F32), "sdn": ((128, PT), F32)}
    nc = build_program(body_ffn0_prep, spec_in, spec_out, n_banks=4)
    shared = {"w_gate": w_gate, "w_up": w_up, "w_down": w_down,
              "c_eyeb": CONST_INPUTS_EYE["c_eyeb"], **CONST_INPUTS}
    res = run_bass_kernel_spmd(nc, [dict(shared) for _ in range(NCORE)],
                               core_ids=list(range(NCORE)))
    return res.results[0]


def run_ffn0(x_fm_cores, prep, clip_x, clip_down):
    spec_in = {"x": ((D, T), F32), "c_eyef": ((128, 128), F32), **_spec_consts()}
    spec_out = {"xn": ((D, T), F32), "lmax": ((1, 1), F32)}
    nc = build_program(body_ffn0_a, spec_in, spec_out)
    shared = {"c_eyef": CONST_INPUTS_EYE["c_eyef"], **CONST_INPUTS}
    res = run_bass_kernel_spmd(nc, [{"x": xc, **shared} for xc in x_fm_cores],
                               core_ids=list(range(NCORE)))
    xns = [r["xn"] for r in res.results]
    gmax8 = np.concatenate([r["lmax"] for r in res.results], axis=1)

    spec_in = {"xn": ((D, T), F32), "gmax8": ((1, NCORE), F32),
               "clip_x": ((1, 1), F32), "wgT": ((D, I_FFN), BF16),
               "wuT": ((D, I_FFN), BF16), "sg": ((128, IT), F32),
               "su": ((128, IT), F32), "c_eyef": ((128, 128), F32), **_spec_consts()}
    spec_out = {"h": ((I_FFN, T), F32), "hmax": ((1, 1), F32)}
    nc = build_program(body_ffn0_b, spec_in, spec_out)
    shared = {"gmax8": gmax8, "clip_x": clip_x.reshape(1, 1),
              "wgT": prep["wgT"], "wuT": prep["wuT"], "sg": prep["sg"],
              "su": prep["su"], "c_eyef": CONST_INPUTS_EYE["c_eyef"], **CONST_INPUTS}
    res = run_bass_kernel_spmd(nc, [{"xn": xn, **shared} for xn in xns],
                               core_ids=list(range(NCORE)))
    hs = [r["h"] for r in res.results]
    hmax8 = np.concatenate([r["hmax"] for r in res.results], axis=1)

    spec_in = {"x": ((D, T), F32), "h": ((I_FFN, T), F32),
               "hmax8": ((1, NCORE), F32), "clip_down": ((1, 1), F32),
               "wdT": ((I_FFN, D), BF16), "sdn": ((128, PT), F32), **_spec_consts()}
    spec_out = {"y": ((D, T), F32)}
    nc = build_program(body_ffn0_c, spec_in, spec_out)
    shared = {"hmax8": hmax8, "clip_down": clip_down.reshape(1, 1),
              "wdT": prep["wdT"], "sdn": prep["sdn"], **CONST_INPUTS}
    res = run_bass_kernel_spmd(
        nc, [{"x": xc, "h": h, **shared} for xc, h in zip(x_fm_cores, hs)],
        core_ids=list(range(NCORE)))
    return [r["y"] for r in res.results]


def kernel(**inputs):
    x = np.ascontiguousarray(inputs["L0_attn_x"], dtype=np.float32)
    x_fm = [_fm(x[b]) for b in range(NCORE)]

    prep = run_ffn0_prep(np.ascontiguousarray(inputs["L0_ffn_w_gate"], dtype=np.float32),
                         np.ascontiguousarray(inputs["L0_ffn_w_up"], dtype=np.float32),
                         np.ascontiguousarray(inputs["L0_ffn_w_down"], dtype=np.float32))
    clip_x = np.asarray(inputs["L0_ffn_clip_x"], dtype=np.float32)
    clip_down = np.asarray(inputs["L0_ffn_clip_down"], dtype=np.float32)

    x_fm = run_ffn0(x_fm, prep, clip_x, clip_down)
    x_fm = run_mamba(x_fm,
                     np.ascontiguousarray(inputs["L0_attn_W_in"], dtype=np.float32),
                     np.ascontiguousarray(inputs["L0_attn_conv_w"], dtype=np.float32),
                     np.ascontiguousarray(inputs["L0_attn_conv_b"], dtype=np.float32),
                     np.ascontiguousarray(inputs["L0_attn_A_log"], dtype=np.float32),
                     np.ascontiguousarray(inputs["L0_attn_D"], dtype=np.float32),
                     np.ascontiguousarray(inputs["L0_attn_W_out"], dtype=np.float32))
    x_fm = run_ffn0(x_fm, prep, clip_x, clip_down)

    l1f = (np.ascontiguousarray(inputs["L1_ffn_W_r"], dtype=np.float32),
           np.ascontiguousarray(inputs["L1_ffn_W_k"], dtype=np.float32),
           np.ascontiguousarray(inputs["L1_ffn_W_v"], dtype=np.float32),
           np.ascontiguousarray(inputs["L1_ffn_mu_r"], dtype=np.float32),
           np.ascontiguousarray(inputs["L1_ffn_mu_k"], dtype=np.float32))
    x_fm = run_l1ffn(x_fm, *l1f)
    x_fm = run_l1attn(x_fm,
                      np.ascontiguousarray(inputs["L1_attn_q_weight"], dtype=np.float32),
                      np.ascontiguousarray(inputs["L1_attn_v_weight"], dtype=np.float32),
                      np.ascontiguousarray(inputs["L1_attn_out_w"], dtype=np.float32),
                      np.ascontiguousarray(inputs["L1_attn_out_b"], dtype=np.float32))
    x_fm = run_l1ffn(x_fm, *l1f)

    return np.stack([np.asarray(y, dtype=np.float32).T for y in x_fm], axis=0)
